# version 01
# baseline (speedup 1.0000x reference)
import numpy as np
import ml_dtypes
from contextlib import ExitStack

import concourse.bass as bass
import concourse.mybir as mybir
from concourse.bass_utils import run_bass_kernel_spmd

F32 = mybir.dt.float32
BF16 = mybir.dt.bfloat16
U32 = mybir.dt.uint32
AF = mybir.ActivationFunctionType
ALU = mybir.AluOpType
AX = mybir.AxisListType
BF = ml_dtypes.bfloat16
ds = bass.ds

T = 4096
TH = 2048
D = 1024
H4 = 4
DH = 256
NG = 8
NCH = 32
NTC = 16
E = 8
FF = 4096
NFC = 32
NLEV = 7
CONST_VALS = [1e-5, 1e-6, 1.0 / 1024.0, 1.0 / 256.0, 10.0, float(64.0 ** (-0.5))]

ENGINES = ("tensor", "vector", "scalar", "gpsimd", "sync")
EIDX = {e: i for i, e in enumerate(ENGINES)}


class _Op:
    __slots__ = ("fn", "deps", "dma", "tag", "needs_inc", "inc_after")

    def __init__(self, fn, deps, dma, tag):
        self.fn = fn
        self.deps = deps
        self.dma = dma
        self.tag = tag
        self.needs_inc = False
        self.inc_after = 0


SERIAL = False
MOE_CW_ONE = False


class Prog:
    def __init__(self):
        self.ops = {e: [] for e in ENGINES}
        self.last_write = {}
        self.readers = {}
        self.fence_toks = []
        self._prev = None

    def set_fence(self):
        self.fence_toks = [(e, len(self.ops[e]) - 1) for e in ENGINES if self.ops[e]]

    def op(self, eng, fn, reads=(), writes=(), dma=False, tag="", extra_deps=()):
        deps = set()
        for r in reads:
            t = self.last_write.get(r)
            if t is not None:
                deps.add(t)
        for w in writes:
            t = self.last_write.get(w)
            if t is not None:
                deps.add(t)
            for t2 in self.readers.get(w, ()):
                deps.add(t2)
        for t in extra_deps:
            if t is not None:
                deps.add(t)
        for t in self.fence_toks:
            deps.add(t)
        if eng == "tensor":
            deps = {d for d in deps if d[0] != "tensor"}
        if SERIAL and self._prev is not None and self._prev != (eng, len(self.ops[eng])):
            deps.add(self._prev)
        deps.discard((eng, len(self.ops[eng])))
        tok = (eng, len(self.ops[eng]))
        self.ops[eng].append(_Op(fn, sorted(deps), dma, tag))
        self._prev = tok
        for r in reads:
            self.readers.setdefault(r, []).append(tok)
        for w in writes:
            self.last_write[w] = tok
            self.readers[w] = []
        return tok

    def t(self, fn, **kw): return self.op("tensor", fn, **kw)
    def v(self, fn, **kw): return self.op("vector", fn, **kw)
    def s(self, fn, **kw): return self.op("scalar", fn, **kw)
    def g(self, fn, **kw): return self.op("gpsimd", fn, **kw)
    def sy(self, fn, **kw): return self.op("sync", fn, **kw)

    def finalize(self, nc, ctx):
        for e in ENGINES:
            for o in self.ops[e]:
                if o.dma:
                    o.needs_inc = True
                for (pe, pi) in o.deps:
                    self.ops[pe][pi].needs_inc = True
        for e in ENGINES:
            c = 0
            for o in self.ops[e]:
                if o.needs_inc:
                    c += 16 if o.dma else 1
                o.inc_after = c
        self.sems = {e: ctx.enter_context(nc.semaphore(f"sem_{e}"))
                     for e in ENGINES if self.ops[e]}
        self.check()

    def check(self):
        done = {e: 0 for e in ENGINES}
        total = sum(len(self.ops[e]) for e in ENGINES)
        ndone = 0
        progress = True
        while progress:
            progress = False
            for e in ENGINES:
                while done[e] < len(self.ops[e]):
                    o = self.ops[e][done[e]]
                    if not all(pi < done[pe] for (pe, pi) in o.deps):
                        break
                    done[e] += 1
                    ndone += 1
                    progress = True
        if ndone != total:
            for e in ENGINES:
                if done[e] < len(self.ops[e]):
                    o = self.ops[e][done[e]]
                    raise RuntimeError(
                        f"deadlock at {e}[{done[e]}] tag={o.tag} deps={o.deps}")

    def emit(self, block):
        op_know = {}
        know = {e: [0] * len(ENGINES) for e in ENGINES}

        def mk_body(eng):
            def body(engine_obj):
                kn = know[eng]
                for i, o in enumerate(self.ops[eng]):
                    for (pe, pi) in o.deps:
                        need = self.ops[pe][pi].inc_after
                        j = EIDX[pe]
                        if need > kn[j]:
                            engine_obj.wait_ge(self.sems[pe], need)
                            kn[j] = need
                            pk = op_know.get((pe, pi))
                            if pk:
                                for jj in range(len(ENGINES)):
                                    if pk[jj] > kn[jj]:
                                        kn[jj] = pk[jj]
                    inst = o.fn(engine_obj)
                    if o.needs_inc:
                        if inst is None:
                            raise RuntimeError(f"{eng}[{i}] {o.tag}: inc needed, no inst")
                        inst.then_inc(self.sems[eng], 16 if o.dma else 1)
                    op_know[(eng, i)] = list(kn)
            return body

        for eng in ENGINES:
            if self.ops[eng]:
                getattr(block, eng)(mk_body(eng))



def mk_ln_scratch(nc, ctx, name, w=1024):
    S = {"_n": name}
    for k, shape in (("s1", [128, 1]), ("nmu", [128, 1]), ("xc", [128, w]),
                     ("xsq", [128, w]), ("ssq", [128, 1]), ("sv", [128, 1]),
                     ("rstd", [128, 1])):
        S[k] = ctx.enter_context(nc.sbuf_tensor(name + "_" + k, shape, F32)).ap()[:, :]
    return S


def emit_ln_chunk(p, x_ap, x_key, scr, out_ap, out_keys, pref, eps=1e-5, n=1024):
    inv_n = 1.0 / n
    nm = scr["_n"]
    p.v(lambda v: v.reduce_sum(scr["s1"], x_ap, axis=AX.X),
        reads=[x_key], writes=[nm + "s1"], tag=pref + "sum")
    p.v(lambda v: v.tensor_scalar(scr["nmu"], scr["s1"], -inv_n, None, op0=ALU.mult),
        reads=[nm + "s1"], writes=[nm + "nmu"], tag=pref + "nmu")
    p.v(lambda v: v.tensor_scalar(scr["xc"], x_ap, scr["nmu"], None, op0=ALU.add),
        reads=[x_key, nm + "nmu"], writes=[nm + "xc"], tag=pref + "xc")
    p.s(lambda s: s.activation(scr["xsq"], scr["xc"], AF.Square, accum_out=scr["ssq"]),
        reads=[nm + "xc"], writes=[nm + "xsq", nm + "ssq"], tag=pref + "sq")
    p.s(lambda s: s.activation(scr["sv"], scr["ssq"], AF.Sqrt, bias=float(eps), scale=inv_n),
        reads=[nm + "ssq", "consts"], writes=[nm + "sv"], tag=pref + "sv")
    p.v(lambda v: v.reciprocal(scr["rstd"], scr["sv"]),
        reads=[nm + "sv"], writes=[nm + "rstd"], tag=pref + "rstd")
    p.v(lambda v: v.tensor_scalar(out_ap, scr["xc"], scr["rstd"], None, op0=ALU.mult),
        reads=[nm + "xc", nm + "rstd"], writes=list(out_keys), tag=pref + "out")
    return nm + "xc", nm + "rstd"


def emit_moe(p, nc, CTX, pref, x_tm, idb, idf, ones_f, cfg):
    kind = cfg["kind"]
    with ExitStack() as C:
        def sbX(name, shape, dt): return C.enter_context(nc.sbuf_tensor(pref + name, shape, dt)).ap()
        def psX(name, shape, dt=F32): return C.enter_context(nc.psum_tensor(pref + name, shape, dt)).ap()
        B = [psX(f"_B{i}", [128, 512]) for i in range(6)]
        Btr = psX("_Btr", [128, 128], BF16)
        Bl = psX("_Bl", [128, 16], F32)
        xln_fm = sbX("_xfm", [128, 8 * TH], BF16)
        xln_c = sbX("_xc", [128, D], BF16)
        wbuf = sbX("_w", [128, NTC * 8], F32)
        p.moe_wbuf = wbuf
        lnscX = mk_ln_scratch(nc, C, pref + "_ln")
        for tc in range(NTC):
            emit_ln_chunk(p, x_tm[:, tc * D:(tc + 1) * D], ("x", tc), lnscX,
                          xln_c[:, :], [pref + "xlnc"], f"{pref}l{tc}")
            for dc in range(8):
                p.t(lambda t, dc=dc: t.transpose(
                    Btr[:, :], xln_c[:, dc * 128:(dc + 1) * 128], idb),
                    reads=[pref + "xlnc", "idb"], writes=[pref + "Btr"], tag=f"{pref}t{tc}{dc}")
                p.v(lambda v, tc=tc, dc=dc: v.tensor_copy(
                    xln_fm[:, dc * TH + tc * 128: dc * TH + (tc + 1) * 128], Btr[:, :]),
                    reads=[pref + "Btr"], writes=[(pref + "fm", dc, tc)], tag=f"{pref}te{tc}{dc}")
        if kind == "glu":
            WrWg = sbX("_wrg", [128, 8 * 16], BF16)
            for dc in range(8):
                p.g(lambda g, dc=dc: g.dma_start(
                    out=WrWg[:, dc * 16: dc * 16 + 8], in_=cfg["wr_in"][dc * 128:(dc + 1) * 128, :]),
                    writes=[(pref + "wr", dc)], dma=True, tag=f"{pref}lwr{dc}")
                p.g(lambda g, dc=dc: g.dma_start(
                    out=WrWg[:, dc * 16 + 8: dc * 16 + 16], in_=cfg["wg_in"][dc * 128:(dc + 1) * 128, :]),
                    writes=[(pref + "wr", dc)], dma=True, tag=f"{pref}lwg{dc}")
            sc8a = sbX("_s8a", [128, 8], F32)
            sc8b = sbX("_s8b", [128, 8], F32)
            sc8c = sbX("_s8c", [128, 8], F32)
            col1 = sbX("_c1", [128, 1], F32)
            col1b = sbX("_c1b", [128, 1], F32)
            col1c = sbX("_c1c", [128, 1], F32)
            col2 = sbX("_c2", [128, 1], F32)
            col3 = sbX("_c3", [128, 1], F32)
            for tc in range(NTC):
                for dc in range(8):
                    p.t(lambda t, tc=tc, dc=dc: t.matmul(
                        Bl[:, :],
                        xln_fm[:, dc * TH + tc * 128: dc * TH + (tc + 1) * 128],
                        WrWg[:, dc * 16:(dc + 1) * 16],
                        start=(dc == 0), stop=(dc == 7)),
                        reads=[(pref + "fm", dc, tc), (pref + "wr", dc)],
                        writes=[pref + "Bl"], tag=f"{pref}rl{tc}{dc}")
                q = f"{pref}r{tc}"
                p.v(lambda v: v.reduce_max(col1[:, :], Bl[:, 0:8], axis=AX.X, negate=True),
                    reads=[pref + "Bl"], writes=[pref + "nmr"], tag=q + "mr")
                p.s(lambda s: s.activation(sc8a[:, :], Bl[:, 0:8], AF.Exp, bias=col1[:, :],
                                           accum_out=col2[:, :]),
                    reads=[pref + "Bl", pref + "nmr"], writes=[pref + "re", pref + "rs"], tag=q + "ex")
                p.v(lambda v: v.reciprocal(col2[:, :], col2[:, :]),
                    reads=[pref + "rs"], writes=[pref + "rs"], tag=q + "rr")
                p.v(lambda v: v.tensor_scalar(sc8a[:, :], sc8a[:, :], col2[:, :], None, op0=ALU.mult),
                    reads=[pref + "re", pref + "rs"], writes=[pref + "re"], tag=q + "rp")
                p.v(lambda v: v.reduce_max(col1b[:, :], Bl[:, 8:16], axis=AX.X, negate=True),
                    reads=[pref + "Bl"], writes=[pref + "nmg"], tag=q + "mg")
                p.v(lambda v: v.tensor_scalar(col1b[:, :], col1b[:, :], 10.0, None, op0=ALU.mult),
                    reads=[pref + "nmg"], writes=[pref + "nmg"], tag=q + "mg10")
                p.s(lambda s: s.activation(sc8b[:, :], Bl[:, 8:16], AF.Exp, bias=col1b[:, :],
                                           scale=10.0, accum_out=col3[:, :]),
                    reads=[pref + "Bl", pref + "nmg", "consts"], writes=[pref + "ge", pref + "gs"], tag=q + "gex")
                p.v(lambda v: v.reciprocal(col3[:, :], col3[:, :]),
                    reads=[pref + "gs"], writes=[pref + "gs"], tag=q + "gr")
                p.v(lambda v: v.tensor_scalar(sc8b[:, :], sc8b[:, :], col3[:, :], None, op0=ALU.mult),
                    reads=[pref + "ge", pref + "gs"], writes=[pref + "ge"], tag=q + "gp")
                p.v(lambda v: v.tensor_tensor(sc8c[:, :], sc8a[:, :], sc8b[:, :], op=ALU.mult),
                    reads=[pref + "re", pref + "ge"], writes=[pref + "c0"], tag=q + "cw0")
                p.v(lambda v: v.reduce_sum(col1c[:, :], sc8c[:, :], axis=AX.X),
                    reads=[pref + "c0"], writes=[pref + "cs"], tag=q + "csum")
                p.v(lambda v: v.tensor_scalar(col1c[:, :], col1c[:, :], 1e-9, None, op0=ALU.add),
                    reads=[pref + "cs"], writes=[pref + "cs"], tag=q + "cse")
                p.v(lambda v: v.reciprocal(col1c[:, :], col1c[:, :]),
                    reads=[pref + "cs"], writes=[pref + "cs"], tag=q + "csr")
                p.v(lambda v, tc=tc: v.tensor_scalar(
                    wbuf[:, tc * 8:(tc + 1) * 8], sc8c[:, :], col1c[:, :], None, op0=ALU.mult),
                    reads=[pref + "c0", pref + "cs"], writes=[(pref + "w", tc)], tag=q + "cw")
        else:
            Wr_bf = sbX("_wrb", [128, 8 * 8], BF16)
            br_bc = sbX("_brbc", [128, 8], F32)
            brrow = sbX("_brrow", [1, 8], F32)
            for dc in range(8):
                p.g(lambda g, dc=dc: g.dma_start(
                    out=Wr_bf[:, dc * 8:(dc + 1) * 8], in_=cfg["wr_in"][dc * 128:(dc + 1) * 128, :]),
                    writes=[(pref + "wr", dc)], dma=True, tag=f"{pref}lwr{dc}")
            p.g(lambda g: g.dma_start(out=brrow, in_=cfg["br_in"]),
                writes=[pref + "brrow"], dma=True, tag=f"{pref}lbr")
            p.t(lambda t: t.matmul(Bl[:, 0:8], ones_f, brrow, start=True, stop=True),
                reads=[pref + "brrow", "ones_f"], writes=[pref + "Bl"], tag=f"{pref}brbc")
            p.v(lambda v: v.tensor_copy(br_bc, Bl[:, 0:8]),
                reads=[pref + "Bl"], writes=[pref + "brbc"], tag=f"{pref}brev")
            lg8 = sbX("_lg8", [128, 8], F32)
            ex8 = sbX("_ex8", [128, 8], F32)
            mk8 = sbX("_mk8", [128, 8], F32)
            colm = sbX("_colm", [128, 1], F32)
            coln = sbX("_coln", [128, 1], F32)
            cols = sbX("_cols", [128, 1], F32)
            for tc in range(NTC):
                for dc in range(8):
                    p.t(lambda t, tc=tc, dc=dc: t.matmul(
                        Bl[:, 0:8],
                        xln_fm[:, dc * TH + tc * 128: dc * TH + (tc + 1) * 128],
                        Wr_bf[:, dc * 8:(dc + 1) * 8],
                        start=(dc == 0), stop=(dc == 7)),
                        reads=[(pref + "fm", dc, tc), (pref + "wr", dc)],
                        writes=[pref + "Bl"], tag=f"{pref}rl{tc}{dc}")
                q = f"{pref}r{tc}"
                p.v(lambda v: v.tensor_tensor(lg8[:, :], Bl[:, 0:8], br_bc, op=ALU.add),
                    reads=[pref + "Bl", pref + "brbc"], writes=[pref + "lg"], tag=q + "lg")
                p.v(lambda v: v.reduce_max(colm[:, :], lg8[:, :], axis=AX.X),
                    reads=[pref + "lg"], writes=[pref + "m"], tag=q + "m")
                p.v(lambda v: v.tensor_scalar(coln[:, :], colm[:, :], -1.0, None, op0=ALU.mult),
                    reads=[pref + "m"], writes=[pref + "nm"], tag=q + "nm")
                p.s(lambda s: s.activation(ex8[:, :], lg8[:, :], AF.Exp, bias=coln[:, :],
                                           accum_out=cols[:, :]),
                    reads=[pref + "lg", pref + "nm"], writes=[pref + "ex", pref + "es"], tag=q + "ex")
                p.v(lambda v: v.reciprocal(cols[:, :], cols[:, :]),
                    reads=[pref + "es"], writes=[pref + "es"], tag=q + "gt")
                p.v(lambda v: v.tensor_scalar(mk8[:, :], lg8[:, :], colm[:, :], None, op0=ALU.is_ge),
                    reads=[pref + "lg", pref + "m"], writes=[pref + "msk"], tag=q + "msk")
                p.v(lambda v, tc=tc: v.tensor_scalar(
                    wbuf[:, tc * 8:(tc + 1) * 8], mk8[:, :], cols[:, :], None, op0=ALU.mult),
                    reads=[pref + "msk", pref + "es"], writes=[(pref + "w", tc)], tag=q + "w")

        h_sb = [sbX("_h0", [128, NFC * 512], BF16), sbX("_h1", [128, NFC * 512], BF16)]
        w1sl = sbX("_w1s", [128, 2 * 1024], BF16)
        wusl = sbX("_wus", [128, 2 * 1024], BF16) if kind == "glu" else None
        w2sl = sbX("_w2s", [128, 2 * 512], BF16)
        gs_sc = [sbX("_gs0", [128, 512], F32), sbX("_gs1", [128, 512], F32)]
        b1_sb = sbX("_b1", [128, NFC], F32) if kind == "relu" else None
        for e in range(E):
            if kind == "relu":
                p.g(lambda g, e=e: g.dma_start(out=b1_sb, in_=cfg["b1_in"][e, :, :]),
                    writes=[pref + "b1"], dma=True, tag=f"{pref}lb1{e}")
            for tsp in range(2):
                for fc in range(NFC):
                    par = fc % 2
                    w1ap = w1sl[:, par * 1024:(par + 1) * 1024]
                    p.g(lambda g, e=e, fc=fc, w1ap=w1ap: g.dma_start(
                        out=w1ap.rearrange("p (dc f) -> p dc f", f=128),
                        in_=cfg["w1_in"][e, :, fc * 128:(fc + 1) * 128]
                        .rearrange("(dc p) f -> p dc f", p=128)),
                        writes=[(pref + "w1s", par)], dma=True, tag=f"{pref}lw1{e}{tsp}{fc}")
                    if kind == "glu":
                        wuap = wusl[:, par * 1024:(par + 1) * 1024]
                        p.g(lambda g, e=e, fc=fc, wuap=wuap: g.dma_start(
                            out=wuap.rearrange("p (dc f) -> p dc f", f=128),
                            in_=cfg["wu_in"][e, :, fc * 128:(fc + 1) * 128]
                            .rearrange("(dc p) f -> p dc f", p=128)),
                            writes=[(pref + "wus", par)], dma=True, tag=f"{pref}lwu{e}{tsp}{fc}")
                    for ts_loc in range(2):
                        ts = tsp * 2 + ts_loc
                        for dc in range(8):
                            p.t(lambda t, dc=dc, ts=ts, ts_loc=ts_loc, w1ap=w1ap: t.matmul(
                                B[ts_loc][:, :],
                                w1ap[:, dc * 128:(dc + 1) * 128],
                                xln_fm[:, dc * TH + ts * 512: dc * TH + (ts + 1) * 512],
                                start=(dc == 0), stop=(dc == 7), skip_group_check=True),
                                reads=[(pref + "w1s", par)] + [(pref + "fm", dc, ts * 4 + i) for i in range(4)],
                                writes=[(pref + "B", ts_loc)], tag=f"{pref}g{e}{fc}{ts_loc}{dc}")
                        if kind == "glu":
                            for dc in range(8):
                                p.t(lambda t, dc=dc, ts=ts, ts_loc=ts_loc, wuap=wuap: t.matmul(
                                    B[2 + ts_loc][:, :],
                                    wuap[:, dc * 128:(dc + 1) * 128],
                                    xln_fm[:, dc * TH + ts * 512: dc * TH + (ts + 1) * 512],
                                    start=(dc == 0), stop=(dc == 7), skip_group_check=True),
                                    reads=[(pref + "wus", par)] + [(pref + "fm", dc, ts * 4 + i) for i in range(4)],
                                    writes=[(pref + "B", 2 + ts_loc)], tag=f"{pref}u{e}{fc}{ts_loc}{dc}")
                    for ts_loc in range(2):
                        if kind == "glu":
                            p.s(lambda s, ts_loc=ts_loc: s.activation(
                                gs_sc[ts_loc][:, :], B[ts_loc][:, :], AF.Silu),
                                reads=[(pref + "B", ts_loc)], writes=[(pref + "gs", ts_loc)],
                                tag=f"{pref}sl{e}{tsp}{fc}{ts_loc}")
                            p.v(lambda v, ts_loc=ts_loc, fc=fc: v.tensor_tensor(
                                h_sb[ts_loc][:, fc * 512:(fc + 1) * 512],
                                gs_sc[ts_loc][:, :], B[2 + ts_loc][:, :], op=ALU.mult),
                                reads=[(pref + "gs", ts_loc), (pref + "B", 2 + ts_loc)],
                                writes=[(pref + "h", ts_loc, fc)], tag=f"{pref}h{e}{tsp}{fc}{ts_loc}")
                        else:
                            p.s(lambda s, ts_loc=ts_loc, fc=fc: s.activation(
                                h_sb[ts_loc][:, fc * 512:(fc + 1) * 512], B[ts_loc][:, :],
                                AF.Relu, bias=b1_sb[:, fc:fc + 1]),
                                reads=[(pref + "B", ts_loc), pref + "b1"],
                                writes=[(pref + "h", ts_loc, fc)],
                                tag=f"{pref}h{e}{tsp}{fc}{ts_loc}")
                for ts_loc in range(2):
                    ts = tsp * 2 + ts_loc
                    for half in range(2):
                        for fc in range(NFC):
                            par = fc % 2
                            w2ap = w2sl[:, par * 512:(par + 1) * 512]
                            p.g(lambda g, e=e, fc=fc, half=half, w2ap=w2ap: g.dma_start(
                                out=w2ap,
                                in_=cfg["w2_in"][e, fc * 128:(fc + 1) * 128, half * 512:(half + 1) * 512]),
                                writes=[(pref + "w2s", par)], dma=True,
                                tag=f"{pref}lw2{e}{ts}{half}{fc}")
                            for tt in range(4):
                                p.t(lambda t, fc=fc, tt=tt, ts_loc=ts_loc, w2ap=w2ap: t.matmul(
                                    B[tt][:, :],
                                    h_sb[ts_loc][:, fc * 512 + tt * 128: fc * 512 + (tt + 1) * 128],
                                    w2ap,
                                    start=(fc == 0), stop=(fc == NFC - 1), skip_group_check=True),
                                    reads=[(pref + "h", ts_loc, fc), (pref + "w2s", par)],
                                    writes=[(pref + "B", tt)], tag=f"{pref}d{e}{ts}{half}{fc}{tt}")
                        for tt in range(4):
                            tcg = ts * 4 + tt
                            wc = 1.0 if MOE_CW_ONE else wbuf[:, tcg * 8 + e: tcg * 8 + e + 1]
                            p.v(lambda v, tt=tt, half=half, tcg=tcg, wc=wc: v.scalar_tensor_tensor(
                                x_tm[:, tcg * D + half * 512: tcg * D + half * 512 + 512],
                                B[tt][:, :], wc,
                                x_tm[:, tcg * D + half * 512: tcg * D + half * 512 + 512],
                                op0=ALU.mult, op1=ALU.add),
                                reads=[(pref + "B", tt), (pref + "w", tcg), ("x", tcg)],
                                writes=[("x", tcg)], tag=f"{pref}f{e}{ts}{tt}{half}")
        if kind == "relu":
            w_fm = sbX("_wfm", [8, NTC * 128], BF16)
            b2_bf = sbX("_b2", [8, D], BF16)
            p.g(lambda g: g.dma_start(out=b2_bf, in_=cfg["b2_in"]),
                writes=[pref + "b2"], dma=True, tag=f"{pref}lb2")
            for tc in range(NTC):
                p.t(lambda t, tc=tc: t.transpose(B[4][0:8, 0:128], wbuf[:, tc * 8:(tc + 1) * 8], idf),
                    reads=[(pref + "w", tc), "idf"], writes=[(pref + "B", 4)], tag=f"{pref}wt{tc}")
                p.v(lambda v, tc=tc: v.tensor_copy(w_fm[:, tc * 128:(tc + 1) * 128], B[4][0:8, 0:128]),
                    reads=[(pref + "B", 4)], writes=[(pref + "wfm", tc)], tag=f"{pref}wte{tc}")
            for tc in range(NTC):
                for half in range(2):
                    p.t(lambda t, tc=tc, half=half: t.matmul(
                        B[5][:, :],
                        w_fm[:, tc * 128:(tc + 1) * 128],
                        b2_bf[:, half * 512:(half + 1) * 512],
                        start=True, stop=True),
                        reads=[(pref + "wfm", tc), pref + "b2"], writes=[(pref + "B", 5)],
                        tag=f"{pref}b2m{tc}{half}")
                    p.v(lambda v, tc=tc, half=half: v.tensor_tensor(
                        x_tm[:, tc * D + half * 512: tc * D + half * 512 + 512],
                        x_tm[:, tc * D + half * 512: tc * D + half * 512 + 512],
                        B[5][:, :], op=ALU.add),
                        reads=[(pref + "B", 5), ("x", tc)], writes=[("x", tc)],
                        tag=f"{pref}b2a{tc}{half}")



def emit_scan_head(p, tl, h, c, bufs):
    (q_fm, k_fm, k_tmc, v_tmc, beta_g, S_all, Sbf_all, Mb, MTb, Pb, PTb,
     t256, cur, D_bf, Wmb, mL, mU, idf, ps_a, ps_b, ps_c, ps_d, ps_trf,
     osq, ssq_c, sv_c, rst_c, grms_bc, o_stage, ostg_key) = bufs
    Qt = [q_fm[:, (2 * h + kh) * 512 + tl * 128: (2 * h + kh) * 512 + (tl + 1) * 128] for kh in range(2)]
    Kt = [k_fm[:, (2 * h + kh) * 512 + tl * 128: (2 * h + kh) * 512 + (tl + 1) * 128] for kh in range(2)]
    Ktm = [k_tmc[:, (2 * h + kh) * 128:(2 * h + kh + 1) * 128] for kh in range(2)]
    Vtm = v_tmc[:, h * 256:(h + 1) * 256]
    Sh = S_all[:, h * 512:(h + 1) * 512]
    Sbh = Sbf_all[:, h * 512:(h + 1) * 512]
    bcol = beta_g[:, tl * 4 + h: tl * 4 + h + 1]
    qk = [("qfm", 2 * h), ("qfm", 2 * h + 1)]
    kk_ = [("kfm", 2 * h), ("kfm", 2 * h + 1)]

    p.t(lambda t: t.matmul(ps_a[:, :], Kt[0], Kt[0], start=True, stop=False),
        reads=[kk_[0]], writes=["ps_a"], tag=f"KK0_{c}_{h}")
    p.t(lambda t: t.matmul(ps_a[:, :], Kt[1], Kt[1], start=False, stop=True),
        reads=[kk_[1]], writes=["ps_a"], tag=f"KK1_{c}_{h}")
    p.v(lambda v: v.scalar_tensor_tensor(Mb[:, :], ps_a[:, :], bcol, mL, op0=ALU.mult, op1=ALU.mult),
        reads=["ps_a", ("beta", tl), "mL"], writes=["Mb"], tag=f"M_{c}_{h}")
    p.t(lambda t: t.transpose(ps_trf[:, :], Mb[:, :], idf),
        reads=["Mb", "idf"], writes=["ps_trf"], tag=f"MT_{c}_{h}")
    p.v(lambda v: v.tensor_copy(MTb[:, :], ps_trf[:, :]),
        reads=["ps_trf"], writes=["MTb"], tag=f"MTe_{c}_{h}")
    p.t(lambda t: t.matmul(ps_c[:, :], Kt[0], Sbh[:, 0:256], start=True, stop=False),
        reads=[kk_[0], ("Sbf", h)], writes=["ps_c"], tag=f"KS0_{c}_{h}")
    p.t(lambda t: t.matmul(ps_c[:, :], Kt[1], Sbh[:, 256:512], start=False, stop=True),
        reads=[kk_[1], ("Sbf", h)], writes=["ps_c"], tag=f"KS1_{c}_{h}")
    p.v(lambda v: v.tensor_tensor(t256[:, :], Vtm, ps_c[:, :], op=ALU.subtract),
        reads=["v_tmc", "ps_c"], writes=["t256"], tag=f"R1_{c}_{h}")
    p.v(lambda v: v.tensor_scalar(cur[0][:, :], t256[:, :], bcol, None, op0=ALU.mult),
        reads=["t256", ("beta", tl)], writes=["cur0b"], tag=f"R2_{c}_{h}")
    p.t(lambda t: t.matmul(ps_d[:, 0:256], MTb[:, :], cur[0][:, :], start=True, stop=True),
        reads=["MTb", "cur0b"], writes=["ps_d"], tag=f"A0_{c}_{h}")
    p.v(lambda v: v.tensor_tensor(cur[1][:, :], cur[0][:, :], ps_d[:, 0:256], op=ALU.subtract),
        reads=["cur0b", "ps_d"], writes=["cur1b"], tag=f"Av0_{c}_{h}")
    cur_k, nxt_k = 1, 0
    Pp, PTp = ("Mb", Mb), ("MTb", MTb)
    pc = 0
    for lev in range(1, NLEV):
        Pc, PTc = (f"P{pc}b", Pb[pc]), (f"PT{pc}b", PTb[pc])
        p.t(lambda t, a=PTp[1], b=Pp[1]: t.matmul(ps_a[:, :], a[:, :], b[:, :], start=True, stop=True),
            reads=[PTp[0], Pp[0]], writes=["ps_a"], tag=f"sq{lev}_{c}_{h}")
        p.v(lambda v, o=Pc[1]: v.tensor_copy(o[:, :], ps_a[:, :]),
            reads=["ps_a"], writes=[Pc[0]], tag=f"sqe{lev}_{c}_{h}")
        p.t(lambda t, a=Pp[1], b=PTp[1]: t.matmul(ps_b[:, :], a[:, :], b[:, :], start=True, stop=True),
            reads=[Pp[0], PTp[0]], writes=["ps_b"], tag=f"sqt{lev}_{c}_{h}")
        p.v(lambda v, o=PTc[1]: v.tensor_copy(o[:, :], ps_b[:, :]),
            reads=["ps_b"], writes=[PTc[0]], tag=f"sqte{lev}_{c}_{h}")
        p.t(lambda t, a=PTc[1], b=cur[cur_k]: t.matmul(ps_d[:, 0:256], a[:, :], b[:, :], start=True, stop=True),
            reads=[PTc[0], f"cur{cur_k}b"], writes=["ps_d"], tag=f"A{lev}_{c}_{h}")
        p.v(lambda v, a=cur[cur_k], o=cur[nxt_k]: v.tensor_tensor(o[:, :], a[:, :], ps_d[:, 0:256], op=ALU.add),
            reads=[f"cur{cur_k}b", "ps_d"], writes=[f"cur{nxt_k}b"], tag=f"Av{lev}_{c}_{h}")
        cur_k, nxt_k = nxt_k, cur_k
        Pp, PTp = Pc, PTc
        pc = 1 - pc
    p.v(lambda v, a=cur[cur_k]: v.tensor_copy(D_bf[:, :], a[:, :]),
        reads=[f"cur{cur_k}b"], writes=["D_bfb"], tag=f"Dbf_{c}_{h}")
    p.t(lambda t: t.matmul(ps_a[:, :], Kt[0], Qt[0], start=True, stop=False),
        reads=[kk_[0], qk[0]], writes=["ps_a"], tag=f"KQ0_{c}_{h}")
    p.t(lambda t: t.matmul(ps_a[:, :], Kt[1], Qt[1], start=False, stop=True),
        reads=[kk_[1], qk[1]], writes=["ps_a"], tag=f"KQ1_{c}_{h}")
    p.v(lambda v: v.tensor_tensor(Wmb[:, :], ps_a[:, :], mU, op=ALU.mult),
        reads=["ps_a", "mU"], writes=["Wmb"], tag=f"Wm_{c}_{h}")
    p.t(lambda t: t.matmul(ps_c[:, :], Qt[0], Sbh[:, 0:256], start=True, stop=False),
        reads=[qk[0], ("Sbf", h)], writes=["ps_c"], tag=f"O0_{c}_{h}")
    p.t(lambda t: t.matmul(ps_c[:, :], Qt[1], Sbh[:, 256:512], start=False, stop=False),
        reads=[qk[1], ("Sbf", h)], writes=["ps_c"], tag=f"O1_{c}_{h}")
    p.t(lambda t: t.matmul(ps_c[:, :], Wmb[:, :], D_bf[:, :], start=False, stop=True),
        reads=["Wmb", "D_bfb"], writes=["ps_c"], tag=f"O2_{c}_{h}")
    p.s(lambda s: s.activation(osq[:, :], ps_c[:, :], AF.Square, accum_out=ssq_c[:, :]),
        reads=["ps_c"], writes=["osq", "ssq_cb"], tag=f"Osq_{c}_{h}")
    p.s(lambda s: s.activation(sv_c[:, :], ssq_c[:, :], AF.Sqrt, bias=1e-5, scale=1.0 / DH),
        reads=["ssq_cb", "consts"], writes=["sv_cb"], tag=f"Osv_{c}_{h}")
    p.v(lambda v: v.reciprocal(rst_c[:, :], sv_c[:, :]),
        reads=["sv_cb"], writes=["rst_cb"], tag=f"Orc_{c}_{h}")
    p.v(lambda v: v.scalar_tensor_tensor(
        o_stage[:, h * 256:(h + 1) * 256], ps_c[:, :], rst_c[:, :], grms_bc,
        op0=ALU.mult, op1=ALU.mult),
        reads=["ps_c", "rst_cb", "grms_bc"], writes=[ostg_key], tag=f"On_{c}_{h}")
    p.t(lambda t: t.matmul(ps_d[:, 0:256], Ktm[0], D_bf[:, :], start=True, stop=True),
        reads=["k_tmc", "D_bfb"], writes=["ps_d"], tag=f"U0_{c}_{h}")
    p.t(lambda t: t.matmul(ps_d[:, 256:512], Ktm[1], D_bf[:, :], start=True, stop=True),
        reads=["k_tmc", "D_bfb"], writes=["ps_d"], tag=f"U1_{c}_{h}")
    p.v(lambda v: v.tensor_tensor(Sh, Sh, ps_d[:, :], op=ALU.add),
        reads=[("S", h), "ps_d"], writes=[("S", h)], tag=f"Su_{c}_{h}")
    p.v(lambda v: v.tensor_copy(Sbh, Sh),
        reads=[("S", h)], writes=[("Sbf", h)], tag=f"Sb_{c}_{h}")



def build_launch1(dbg=False):
    nc = bass.Bass("TRN2", target_bir_lowering=False)
    def inp(name, shape, dt=F32):
        return nc.dram_tensor(name, shape, dt, kind="ExternalInput").ap()
    def outp(name, shape, dt=F32):
        return nc.dram_tensor(name, shape, dt, kind="ExternalOutput").ap()

    x_in = inp("x_b", [T, D])
    xh_in = inp("x_half", [TH, D])
    s_in = inp("s_sel", [1, 1], U32)
    Wq_in = inp("Wq", [D, D]); Wk_in = inp("Wk", [D, D]); Wv_in = inp("Wv", [D, D])
    Wb_in = inp("Wb", [D, 4])
    cq_in = inp("convq", [D, 4]); ck_in = inp("convk", [D, 4]); cv_in = inp("convv", [D, 4])
    grms_in = inp("g_rms", [1, DH])
    Wo_in = inp("Wo", [D, D])
    lng_in = inp("ln_g", [1, D]); lnb_in = inp("ln_b", [1, D])
    wr_in = inp("w_router", [D, E]); wg_in = inp("w_gating", [D, E])
    wgate_in = inp("w_gate", [E, D, FF]); wup_in = inp("w_up", [E, D, FF])
    wdown_in = inp("w_down", [E, FF, D])
    wd1_in = inp("W_down1", [D, D])
    mL_in = inp("maskL", [128, 128]); mU_in = inp("maskU", [128, 128])
    idf_in = inp("idf", [128, 128]); idb_in = inp("idb", [128, 128], BF16)
    consts_in = inp("consts", [128, len(CONST_VALS)])

    x2_out = outp("x2_half", [TH, D])
    xd_out = outp("xd_half", [TH, D], BF16)
    if dbg:
        o_dbg = outp("o_dbg", [T, D], BF16)
        x1_dbg = outp("x1_dbg", [TH, D])

    o_scr = nc.dram_tensor("o_scr", [T, D], BF16).ap()

    p = Prog()
    with ExitStack() as CTX:
        def sb(name, shape, dt): return CTX.enter_context(nc.sbuf_tensor(name, shape, dt)).ap()
        mL = sb("mL", [128, 128], F32); mU = sb("mU", [128, 128], F32)
        idf = sb("idf_s", [128, 128], F32); idb = sb("idb_s", [128, 128], BF16)
        ones_f = sb("ones_f", [1, 128], F32)
        ones_c = sb("ones_c", [128, 1], F32)
        p.g(lambda g: g.dma_start(out=mL, in_=mL_in), writes=["mL"], dma=True, tag="ldmL")
        p.g(lambda g: g.dma_start(out=mU, in_=mU_in), writes=["mU"], dma=True, tag="ldmU")
        p.g(lambda g: g.dma_start(out=idf, in_=idf_in), writes=["idf"], dma=True, tag="ldidf")
        p.g(lambda g: g.dma_start(out=idb, in_=idb_in), writes=["idb"], dma=True, tag="ldidb")
        p.g(lambda g: g.memset(ones_f, 1.0), writes=["ones_f"], tag="onesf")
        p.g(lambda g: g.memset(ones_c, 1.0), writes=["ones_c"], tag="onesc")
        const_sb = sb("const_sb", [128, len(CONST_VALS)], F32)
        p.g(lambda g: g.dma_start(out=const_sb, in_=consts_in), writes=["consts"], dma=True, tag="ldconsts")
        for _i, _v in enumerate(CONST_VALS):
            nc.const_aps.register(F32, _v, const_sb[:, _i:_i + 1])

        with ExitStack() as C1:
            def sb1(name, shape, dt): return C1.enter_context(nc.sbuf_tensor(name, shape, dt)).ap()
            def ps1_(name, shape, dt=F32): return C1.enter_context(nc.psum_tensor(name, shape, dt)).ap()
            Wq_bf = sb1("Wq_bf", [128, 8 * D], BF16)
            Wk_bf = sb1("Wk_bf", [128, 8 * D], BF16)
            Wv_bf = sb1("Wv_bf", [128, 8 * D], BF16)
            Wb_bf = sb1("Wb_bf", [128, 32], BF16)
            conv_sb = {"q": sb1("cq", [128, 32], F32), "k": sb1("ck", [128, 32], F32),
                       "v": sb1("cv", [128, 32], F32)}
            grms_row = sb1("grms_row", [1, DH], F32)
            grms_bc = sb1("grms_bc", [128, DH], F32)
            x_gw = sb1("x_gw", [128, 4 * D], F32)
            xln_c = sb1("xln_c", [128, D], BF16)
            xln_fm = sb1("xln_fm", [128, 8 * 512], BF16)
            q_fm = sb1("q_fm", [128, 8 * 512], BF16)
            k_fm = sb1("k_fm", [128, 8 * 512], BF16)
            v_fm = sb1("v_fm", [128, 8 * 512], BF16)
            beta_g = sb1("beta_g", [128, 16], F32)
            k_tmc = sb1("k_tmc", [128, D], BF16)
            v_tmc = sb1("v_tmc", [128, D], BF16)
            ext = sb1("ext", [128, 516], F32)
            tails = sb1("tails", [128, 72], F32)
            cacc = [sb1("cacc0", [128, 512], F32), sb1("cacc1", [128, 512], F32)]
            sl = [sb1("sl0", [128, 512], F32), sb1("sl1", [128, 512], F32)]
            sqb = sb1("sqb", [128, 512], F32)
            rs_row = sb1("rs_row", [1, 512], F32)
            sv_row = sb1("sv_row", [1, 512], F32)
            S_all = sb1("S_all", [128, H4 * 512], F32)
            Sbf_all = sb1("Sbf_all", [128, H4 * 512], BF16)
            Mb = sb1("Mb", [128, 128], F32); MTb = sb1("MTb", [128, 128], F32)
            Pb = [sb1("P0b", [128, 128], F32), sb1("P1b", [128, 128], F32)]
            PTb = [sb1("PT0b", [128, 128], F32), sb1("PT1b", [128, 128], F32)]
            t256 = sb1("t256", [128, 256], F32)
            cur = [sb1("cur0b", [128, 256], F32), sb1("cur1b", [128, 256], F32)]
            D_bf = sb1("D_bfb", [128, 256], BF16)
            Wmb = sb1("Wmb", [128, 128], BF16)
            ssq_c = sb1("ssq_c", [128, 1], F32)
            sv_c = sb1("sv_c", [128, 1], F32)
            rst_c = sb1("rst_c", [128, 1], F32)
            osq = sb1("osq", [128, 256], F32)
            o_stage = [sb1("o_st0", [128, D], BF16), sb1("o_st1", [128, D], BF16)]
            lnsc = mk_ln_scratch(nc, C1, "p1ln")

            ps_p = ps1_("ps_p", [128, 512])
            ps_bc = ps1_("ps_bc", [128, 512])
            ps_tr = ps1_("ps_tr", [128, 128], BF16)
            ps_trf = ps1_("ps_trf", [128, 128])
            ps_a = ps1_("ps_a", [128, 128])
            ps_b = ps1_("ps_b", [128, 128])
            ps_c = ps1_("ps_c", [128, 256])
            ps_d = ps1_("ps_d", [128, 512])

            for dc in range(8):
                for nm, Wbuf, Win in (("q", Wq_bf, Wq_in), ("k", Wk_bf, Wk_in), ("v", Wv_bf, Wv_in)):
                    p.g(lambda g, dc=dc, Wbuf=Wbuf, Win=Win: g.dma_start(
                        out=Wbuf[:, dc * D:(dc + 1) * D], in_=Win[dc * 128:(dc + 1) * 128, :]),
                        writes=[(f"W{nm}", dc)], dma=True, tag=f"ldW{nm}{dc}")
                p.g(lambda g, dc=dc: g.dma_start(
                    out=Wb_bf[:, dc * 4:(dc + 1) * 4], in_=Wb_in[dc * 128:(dc + 1) * 128, :]),
                    writes=[("Wb", dc)], dma=True, tag=f"ldWb{dc}")
                for nm, Cin in (("q", cq_in), ("k", ck_in), ("v", cv_in)):
                    p.g(lambda g, dc=dc, nm=nm, Cin=Cin: g.dma_start(
                        out=conv_sb[nm][:, dc * 4:(dc + 1) * 4], in_=Cin[dc * 128:(dc + 1) * 128, :]),
                        writes=[(f"conv{nm}", dc)], dma=True, tag=f"ldc{nm}{dc}")
            p.g(lambda g: g.dma_start(out=grms_row, in_=grms_in), writes=["grms_row"], dma=True, tag="ldgr")
            p.t(lambda t: t.matmul(ps_bc[:, 0:DH], ones_f, grms_row, start=True, stop=True),
                reads=["ones_f", "grms_row"], writes=["ps_bc"], tag="grbc")
            p.v(lambda v: v.tensor_copy(grms_bc, ps_bc[:, 0:DH]),
                reads=["ps_bc"], writes=["grms_bc"], tag="grbcev")
            p.g(lambda g: g.memset(tails, 0.0), writes=["tails"], tag="tails0")
            p.g(lambda g: g.memset(S_all, 0.0), writes=[("S", h) for h in range(H4)], tag="S0")
            p.g(lambda g: g.memset(Sbf_all, 0.0), writes=[("Sbf", h) for h in range(H4)], tag="Sbf0")

            PROJ = {"q": (Wq_bf, q_fm, True), "k": (Wk_bf, k_fm, True), "v": (Wv_bf, v_fm, False)}

            for g_i in range(NG):
                p.sy(lambda s, g_i=g_i: s.dma_start(
                    out=x_gw.rearrange("p (tl d) -> p tl d", d=D),
                    in_=x_in[g_i * 512:(g_i + 1) * 512, :]
                    .rearrange("(tl p) d -> p tl d", p=128)),
                    writes=["x_gw"], dma=True, tag=f"ldxg{g_i}")
                for tl in range(4):
                    emit_ln_chunk(p, x_gw[:, tl * D:(tl + 1) * D], "x_gw", lnsc,
                                  xln_c[:, :], ["xln_c"], f"g{g_i}t{tl}")
                    for dc in range(8):
                        p.t(lambda t, dc=dc: t.transpose(
                            ps_tr[:, :], xln_c[:, dc * 128:(dc + 1) * 128], idb),
                            reads=["xln_c", "idb"], writes=["ps_tr"], tag=f"xt{g_i}{tl}{dc}")
                        p.v(lambda v, dc=dc, tl=tl: v.tensor_copy(
                            xln_fm[:, dc * 512 + tl * 128: dc * 512 + (tl + 1) * 128], ps_tr[:, :]),
                            reads=["ps_tr"], writes=[("xfm", dc, tl)], tag=f"xte{g_i}{tl}{dc}")
                    for dc in range(8):
                        p.t(lambda t, dc=dc, tl=tl: t.matmul(
                            ps_bc[0:128, 0:4],
                            xln_fm[:, dc * 512 + tl * 128: dc * 512 + (tl + 1) * 128],
                            Wb_bf[:, dc * 4:(dc + 1) * 4],
                            start=(dc == 0), stop=(dc == 7)),
                            reads=[("xfm", dc, tl), ("Wb", dc)], writes=["ps_bc"], tag=f"bmm{g_i}{tl}{dc}")
                    p.s(lambda s, tl=tl: s.activation(
                        beta_g[:, tl * 4:(tl + 1) * 4], ps_bc[0:128, 0:4], AF.Sigmoid),
                        reads=["ps_bc"], writes=[("beta", tl)], tag=f"bsig{g_i}{tl}")
                for nm in ("q", "k", "v"):
                    Wbuf, outbuf, do_norm = PROJ[nm]
                    for dco in range(8):
                        pp = dco % 2
                        for dci in range(8):
                            p.t(lambda t, dci=dci, dco=dco, Wbuf=Wbuf: t.matmul(
                                ps_p[:, :],
                                Wbuf[:, dci * D + dco * 128: dci * D + (dco + 1) * 128],
                                xln_fm[:, dci * 512:(dci + 1) * 512],
                                start=(dci == 0), stop=(dci == 7)),
                                reads=[(f"W{nm}", dci)] + [("xfm", dci, tl) for tl in range(4)],
                                writes=["ps_p"], tag=f"p{nm}{g_i}{dco}{dci}")
                        tslot = ("q", "k", "v").index(nm) * 24 + dco * 3
                        p.v(lambda v, tslot=tslot: v.tensor_copy(
                            ext[:, 0:3], tails[:, tslot:tslot + 3]),
                            reads=["tails"], writes=["ext_h"], tag=f"ti{nm}{g_i}{dco}")
                        p.v(lambda v: v.tensor_copy(ext[:, 3:515], ps_p[:, :]),
                            reads=["ps_p"], writes=["ext"], tag=f"pe{nm}{g_i}{dco}")
                        p.v(lambda v, tslot=tslot: v.tensor_copy(
                            tails[:, tslot:tslot + 3], ext[:, 512:515]),
                            reads=["ext"], writes=["tails"], tag=f"to{nm}{g_i}{dco}")
                        cv = conv_sb[nm]
                        p.v(lambda v, dco=dco, cv=cv: v.tensor_scalar(
                            cacc[0][:, :], ext[:, 0:512], cv[:, dco * 4:dco * 4 + 1], None, op0=ALU.mult),
                            reads=["ext", "ext_h", (f"conv{nm}", dco)],
                            writes=["cacc0"], tag=f"c0{nm}{g_i}{dco}")
                        for j in (1, 2, 3):
                            src, dst = cacc[(j + 1) % 2], cacc[j % 2]
                            p.v(lambda v, j=j, dco=dco, cv=cv, src=src, dst=dst: v.scalar_tensor_tensor(
                                dst[:, :], ext[:, j:j + 512], cv[:, dco * 4 + j:dco * 4 + j + 1],
                                src[:, :], op0=ALU.mult, op1=ALU.add),
                                reads=["ext", "ext_h", f"cacc{(j + 1) % 2}", (f"conv{nm}", dco)],
                                writes=[f"cacc{j % 2}"], tag=f"c{j}{nm}{g_i}{dco}")
                        if nm == "v":
                            p.s(lambda s, dco=dco: s.activation(
                                v_fm[:, dco * 512:(dco + 1) * 512], cacc[1][:, :], AF.Silu),
                                reads=["cacc1"], writes=[("vfm", dco)], tag=f"sl{nm}{g_i}{dco}")
                        else:
                            p.s(lambda s, pp=pp: s.activation(sl[pp][:, :], cacc[1][:, :], AF.Silu),
                                reads=["cacc1"], writes=[f"sl{pp}"], tag=f"sl{nm}{g_i}{dco}")
                            p.s(lambda s, pp=pp: s.activation(sqb[:, :], sl[pp][:, :], AF.Square),
                                reads=[f"sl{pp}"], writes=["sqb"], tag=f"sq{nm}{g_i}{dco}")
                            p.t(lambda t, pp=pp: t.matmul(
                                ps_bc[0:1, 0:512], ones_c, sqb[:, :],
                                start=(pp == 0), stop=(pp == 1)),
                                reads=["sqb", "ones_c"], writes=["ps_bc"], tag=f"nrm{nm}{g_i}{dco}")
                            if pp == 1:
                                p.s(lambda s: s.activation(sv_row[:, :], ps_bc[0:1, 0:512], AF.Sqrt, bias=1e-6),
                                    reads=["ps_bc", "consts"], writes=["sv_row"], tag=f"nsq{nm}{g_i}{dco}")
                                p.v(lambda v: v.reciprocal(rs_row[:, :], sv_row[:, :]),
                                    reads=["sv_row"], writes=["rs_row"], tag=f"nrc{nm}{g_i}{dco}")
                                p.t(lambda t: t.matmul(ps_bc[:, :], ones_f, rs_row[:, :], start=True, stop=True),
                                    reads=["rs_row", "ones_f"], writes=["ps_bc"], tag=f"nbc{nm}{g_i}{dco}")
                                for q_dc, q_pp in ((dco - 1, 0), (dco, 1)):
                                    p.v(lambda v, q_dc=q_dc, q_pp=q_pp, outbuf=outbuf: v.tensor_tensor(
                                        outbuf[:, q_dc * 512:(q_dc + 1) * 512], sl[q_pp][:, :],
                                        ps_bc[:, :], op=ALU.mult),
                                        reads=[f"sl{q_pp}", "ps_bc"],
                                        writes=[(f"{nm}fm", q_dc)], tag=f"nn{nm}{g_i}{q_dc}")
                for tl in range(4):
                    c_glob = g_i * 4 + tl
                    for dc in range(8):
                        p.t(lambda t, dc=dc, tl=tl: t.transpose(
                            ps_tr[:, :], k_fm[:, dc * 512 + tl * 128: dc * 512 + (tl + 1) * 128], idb),
                            reads=[("kfm", dc), "idb"], writes=["ps_tr"], tag=f"ktt{c_glob}{dc}")
                        p.v(lambda v, dc=dc: v.tensor_copy(
                            k_tmc[:, dc * 128:(dc + 1) * 128], ps_tr[:, :]),
                            reads=["ps_tr"], writes=["k_tmc"], tag=f"kte{c_glob}{dc}")
                        p.t(lambda t, dc=dc, tl=tl: t.transpose(
                            ps_tr[:, :], v_fm[:, dc * 512 + tl * 128: dc * 512 + (tl + 1) * 128], idb),
                            reads=[("vfm", dc), "idb"], writes=["ps_tr"], tag=f"vtt{c_glob}{dc}")
                        p.v(lambda v, dc=dc: v.tensor_copy(
                            v_tmc[:, dc * 128:(dc + 1) * 128], ps_tr[:, :]),
                            reads=["ps_tr"], writes=["v_tmc"], tag=f"vte{c_glob}{dc}")
                    for h in range(H4):
                        emit_scan_head(p, tl, h, c_glob,
                                       (q_fm, k_fm, k_tmc, v_tmc, beta_g, S_all, Sbf_all,
                                        Mb, MTb, Pb, PTb, t256, cur, D_bf, Wmb, mL, mU, idf,
                                        ps_a, ps_b, ps_c, ps_d, ps_trf, osq, ssq_c, sv_c,
                                        rst_c, grms_bc, o_stage[tl % 2], ("ostg", tl % 2)))
                    p.sy(lambda s, c_glob=c_glob, tl=tl: s.dma_start(
                        out=o_scr[c_glob * 128:(c_glob + 1) * 128, :], in_=o_stage[tl % 2][:, :]),
                        reads=[("ostg", tl % 2)], writes=[("oscr", c_glob)], dma=True, tag=f"sto{c_glob}")
        p.set_fence()

        x_tm = sb("x_tm", [128, NTC * D], F32)
        p.sy(lambda s: s.dma_start(
            out=x_tm.rearrange("p (tc d) -> p tc d", d=D),
            in_=xh_in.rearrange("(tc p) d -> p tc d", p=128)),
            writes=[("x", tc) for tc in range(NTC)], dma=True, tag="ldxh")

        with ExitStack() as C2:
            def sb2(name, shape, dt): return C2.enter_context(nc.sbuf_tensor(name, shape, dt)).ap()
            def ps2_(name, shape, dt=F32): return C2.enter_context(nc.psum_tensor(name, shape, dt)).ap()
            o_half = sb2("o_half", [128, NTC * D], BF16)
            o_fm = sb2("o_fm", [128, 8 * TH], BF16)
            Wo_bf = sb2("Wo_bf", [128, 8 * D], BF16)
            lng_bc = sb2("lng_bc", [128, D], F32)
            lnb_bc = sb2("lnb_bc", [128, D], F32)
            lrow = sb2("lrow", [1, D], F32)
            t1b = sb2("t1b", [128, D], F32)
            t2b = sb2("t2b", [128, D], F32)
            lnsc2 = mk_ln_scratch(nc, C2, "p2ln")
            ps_y = ps2_("ps_y", [128, D])
            ps_tr2 = ps2_("ps_tr2", [128, 128], BF16)
            ps_r2 = ps2_("ps_r2", [128, D])

            for dc in range(8):
                p.g(lambda g, dc=dc: g.dma_start(
                    out=Wo_bf[:, dc * D:(dc + 1) * D], in_=Wo_in[dc * 128:(dc + 1) * 128, :]),
                    writes=[("Wo", dc)], dma=True, tag=f"ldWo{dc}")
            p.g(lambda g: g.dma_start(out=lrow, in_=lng_in), writes=["lrow"], dma=True, tag="ldlng")
            for half in range(2):
                p.t(lambda t, half=half: t.matmul(
                    ps_r2[:, half * 512:(half + 1) * 512], ones_f,
                    lrow[:, half * 512:(half + 1) * 512], start=True, stop=True),
                    reads=["lrow", "ones_f"], writes=["ps_r2"], tag=f"lngbc{half}")
            p.v(lambda v: v.tensor_copy(lng_bc, ps_r2),
                reads=["ps_r2"], writes=["lng_bc"], tag="lngev")
            p.g(lambda g: g.dma_start(out=lrow, in_=lnb_in),
                reads=["ps_r2"], writes=["lrow"], dma=True, tag="ldlnb")
            for half in range(2):
                p.t(lambda t, half=half: t.matmul(
                    ps_r2[:, half * 512:(half + 1) * 512], ones_f,
                    lrow[:, half * 512:(half + 1) * 512], start=True, stop=True),
                    reads=["lrow", "ones_f"], writes=["ps_r2"], tag=f"lnbbc{half}")
            p.v(lambda v: v.tensor_copy(lnb_bc, ps_r2),
                reads=["ps_r2"], writes=["lnb_bc"], tag="lnbev")

            s_sv = sb2("s_sv", [1, 1], U32)
            p.g(lambda g: g.dma_start(out=s_sv, in_=s_in), writes=["s_sv"], dma=True, tag="ldssv")
            cell = {}
            def ld_s(g):
                reg = g.alloc_register("sreg")
                g.load(reg, s_sv[0:1, 0:1])
                cell["s"] = g.snap(reg, min_val=0, max_val=1)
                return g.nop()
            p.g(ld_s, reads=["s_sv"], writes=["sreg"], tag="ldsreg")
            p.g(lambda g: g.dma_start(
                out=o_half.rearrange("p (tc d) -> p tc d", d=D),
                in_=o_scr.rearrange("(c p) d -> p c d", p=128)[:, ds(cell["s"] * NTC, NTC), :]),
                reads=["sreg"] + [("oscr", c) for c in range(NCH)],
                writes=[("oh", tc) for tc in range(NTC)], dma=True, tag="ldoh")
            for tc in range(NTC):
                for dc in range(8):
                    p.t(lambda t, tc=tc, dc=dc: t.transpose(
                        ps_tr2[:, :], o_half[:, tc * D + dc * 128: tc * D + (dc + 1) * 128], idb),
                        reads=[("oh", tc), "idb"], writes=["ps_tr2"], tag=f"ot{tc}{dc}")
                    p.v(lambda v, tc=tc, dc=dc: v.tensor_copy(
                        o_fm[:, dc * TH + tc * 128: dc * TH + (tc + 1) * 128], ps_tr2[:, :]),
                        reads=["ps_tr2"], writes=[("ofm", dc, tc)], tag=f"ote{tc}{dc}")
            for tc in range(NTC):
                for half in range(2):
                    for dc in range(8):
                        p.t(lambda t, tc=tc, dc=dc, half=half: t.matmul(
                            ps_y[:, half * 512:(half + 1) * 512],
                            o_fm[:, dc * TH + tc * 128: dc * TH + (tc + 1) * 128],
                            Wo_bf[:, dc * D + half * 512: dc * D + half * 512 + 512],
                            start=(dc == 0), stop=(dc == 7)),
                            reads=[("ofm", dc, tc), ("Wo", dc)], writes=["ps_y"], tag=f"yo{tc}{half}{dc}")
                xap = x_tm[:, tc * D:(tc + 1) * D]
                pref = f"p2a{tc}"
                xck, rstk = emit_ln_partial(p, xap, ("x", tc), lnsc2, pref)
                p.v(lambda v: v.scalar_tensor_tensor(
                    t1b[:, :], lnsc2["xc"], lnsc2["rstd"], ps_y[:, :], op0=ALU.mult, op1=ALU.add),
                    reads=[xck, rstk, "ps_y"], writes=["t1b"], tag=pref + "t1")
                pref = f"p2b{tc}"
                xck, rstk = emit_ln_partial(p, t1b[:, :], "t1b", lnsc2, pref)
                p.v(lambda v: v.tensor_scalar(t2b[:, :], lnsc2["xc"], lnsc2["rstd"], None, op0=ALU.mult),
                    reads=[xck, rstk], writes=["t2b"], tag=pref + "ap")
                p.v(lambda v: v.tensor_tensor(t2b[:, :], t2b[:, :], lng_bc, op=ALU.mult),
                    reads=["t2b", "lng_bc"], writes=["t2b"], tag=pref + "g")
                p.v(lambda v: v.tensor_tensor(t2b[:, :], t2b[:, :], lnb_bc, op=ALU.add),
                    reads=["t2b", "lnb_bc"], writes=["t2b"], tag=pref + "b")
                p.v(lambda v, xap=xap: v.tensor_tensor(xap, xap, t2b[:, :], op=ALU.add),
                    reads=[("x", tc), "t2b"], writes=[("x", tc)], tag=pref + "res")
            if dbg:
                p.dbg_toks = []
                for c in range(NCH):
                    p.dbg_toks.append(p.sy(lambda s, c=c: s.dma_start(
                        out=o_dbg[c * 128:(c + 1) * 128, :], in_=o_scr[c * 128:(c + 1) * 128, :]),
                        reads=[("oscr", c)], writes=[("odbg", c)], dma=True, tag=f"odbg{c}"))
                for tc in range(NTC):
                    p.dbg_toks.append(p.sy(lambda s, tc=tc: s.dma_start(
                        out=x1_dbg[tc * 128:(tc + 1) * 128, :], in_=x_tm[:, tc * D:(tc + 1) * D]),
                        reads=[("x", tc)], writes=[("x1dbg", tc)], dma=True, tag=f"x1dbg{tc}"))
        p.set_fence()

        emit_moe(p, nc, CTX, "l0f", x_tm, idb, idf, ones_f,
                 dict(kind="glu", wr_in=wr_in, wg_in=wg_in,
                      w1_in=wgate_in, wu_in=wup_in, w2_in=wdown_in,
                      br_in=None, b1_in=None, b2_in=None))
        p.set_fence()

        with ExitStack() as C4:
            def sb4(name, shape, dt): return C4.enter_context(nc.sbuf_tensor(name, shape, dt)).ap()
            def ps4_(name, shape, dt=F32): return C4.enter_context(nc.psum_tensor(name, shape, dt)).ap()
            xd_fm = sb4("p4_fm", [128, 8 * TH], BF16)
            Wd1_bf = sb4("Wd1_bf", [128, 8 * D], BF16)
            xln_c4 = sb4("xln_c4", [128, D], BF16)
            xd_stage = sb4("xd_stage", [128, D], BF16)
            lnsc4 = mk_ln_scratch(nc, C4, "p4ln")
            ps_y4 = ps4_("ps_y4", [128, D])
            ps_tr4 = ps4_("ps_tr4", [128, 128], BF16)
            for dc in range(8):
                p.g(lambda g, dc=dc: g.dma_start(
                    out=Wd1_bf[:, dc * D:(dc + 1) * D], in_=wd1_in[dc * 128:(dc + 1) * 128, :]),
                    writes=[("Wd1", dc)], dma=True, tag=f"ldWd1{dc}")
            st_toks = []
            for tc in range(NTC):
                emit_ln_chunk(p, x_tm[:, tc * D:(tc + 1) * D], ("x", tc), lnsc4,
                              xln_c4[:, :], ["xln_c4"], f"p4l{tc}")
                for dc in range(8):
                    p.t(lambda t, dc=dc: t.transpose(
                        ps_tr4[:, :], xln_c4[:, dc * 128:(dc + 1) * 128], idb),
                        reads=["xln_c4", "idb"], writes=["ps_tr4"], tag=f"p4t{tc}{dc}")
                    p.v(lambda v, tc=tc, dc=dc: v.tensor_copy(
                        xd_fm[:, dc * TH + tc * 128: dc * TH + (tc + 1) * 128], ps_tr4[:, :]),
                        reads=["ps_tr4"], writes=[("p4fm", dc, tc)], tag=f"p4te{tc}{dc}")
                st_toks.append(p.sy(lambda s, tc=tc: s.dma_start(
                    out=x2_out[tc * 128:(tc + 1) * 128, :], in_=x_tm[:, tc * D:(tc + 1) * D]),
                    reads=[("x", tc)], dma=True, tag=f"stx2{tc}"))
            for tc in range(NTC):
                for half in range(2):
                    for dc in range(8):
                        p.t(lambda t, tc=tc, dc=dc, half=half: t.matmul(
                            ps_y4[:, half * 512:(half + 1) * 512],
                            xd_fm[:, dc * TH + tc * 128: dc * TH + (tc + 1) * 128],
                            Wd1_bf[:, dc * D + half * 512: dc * D + half * 512 + 512],
                            start=(dc == 0), stop=(dc == 7)),
                            reads=[("p4fm", dc, tc), ("Wd1", dc)], writes=["ps_y4"], tag=f"xd{tc}{half}{dc}")
                p.v(lambda v: v.tensor_copy(xd_stage[:, :], ps_y4[:, :]),
                    reads=["ps_y4"], writes=["xd_stage"], tag=f"xde{tc}")
                st_toks.append(p.sy(lambda s, tc=tc: s.dma_start(
                    out=xd_out[tc * 128:(tc + 1) * 128, :], in_=xd_stage[:, :]),
                    reads=["xd_stage"], dma=True, tag=f"stxd{tc}"))
            p.sy(lambda s: s.nop(), extra_deps=st_toks + getattr(p, "dbg_toks", []), tag="fin")

        p.finalize(nc, CTX)
        with nc.Block() as block:
            p.emit(block)
    return nc


def emit_ln_partial(p, x_ap, x_key, scr, pref, n=1024, eps=1e-5):
    inv_n = 1.0 / n
    nm = scr["_n"]
    p.v(lambda v: v.reduce_sum(scr["s1"], x_ap, axis=AX.X),
        reads=[x_key], writes=[nm + "s1"], tag=pref + "sum")
    p.v(lambda v: v.tensor_scalar(scr["nmu"], scr["s1"], -inv_n, None, op0=ALU.mult),
        reads=[nm + "s1"], writes=[nm + "nmu"], tag=pref + "nmu")
    p.v(lambda v: v.tensor_scalar(scr["xc"], x_ap, scr["nmu"], None, op0=ALU.add),
        reads=[x_key, nm + "nmu"], writes=[nm + "xc"], tag=pref + "xc")
    p.s(lambda s: s.activation(scr["xsq"], scr["xc"], AF.Square, accum_out=scr["ssq"]),
        reads=[nm + "xc"], writes=[nm + "xsq", nm + "ssq"], tag=pref + "sq")
    p.s(lambda s: s.activation(scr["sv"], scr["ssq"], AF.Sqrt, bias=float(eps), scale=inv_n),
        reads=[nm + "ssq", "consts"], writes=[nm + "sv"], tag=pref + "sv")
    p.v(lambda v: v.reciprocal(scr["rstd"], scr["sv"]),
        reads=[nm + "sv"], writes=[nm + "rstd"], tag=pref + "rstd")
    return nm + "xc", nm + "rstd"



def build_launch2(dbg=False, phases=("q1", "q2", "q3")):
    nc = bass.Bass("TRN2", target_bir_lowering=False)
    def inp(name, shape, dt=F32):
        return nc.dram_tensor(name, shape, dt, kind="ExternalInput").ap()
    def outp(name, shape, dt=F32):
        return nc.dram_tensor(name, shape, dt, kind="ExternalOutput").ap()

    xd_in = inp("xd_full", [T, D], BF16)
    x2_in = inp("x2_half", [TH, D])
    tk_in = inp("tokeysT", [16, 64, 64])
    tq_in = inp("toqueriesT", [16, 64, 64])
    tv_in = inp("tovaluesT", [16, 64, 64])
    wuni_in = inp("W_unify", [D, D])
    bu_in = inp("b_unify", [1, D])
    wr1_in = inp("Wr1", [D, E])
    br1_in = inp("br1", [1, E])
    w11_in = inp("W11", [E, D, FF])
    b11_in = inp("b11", [E, 128, 32])
    w21_in = inp("W21", [E, FF, D])
    b21_in = inp("b21", [E, D])
    mL_in = inp("maskL", [128, 128]); mU_in = inp("maskU", [128, 128])
    idf_in = inp("idf", [128, 128]); idb_in = inp("idb", [128, 128], BF16)
    consts_in = inp("consts", [128, len(CONST_VALS)])
    x4_out = outp("x4_half", [TH, D])
    if dbg:
        on_dbg = outp("on_dbg", [TH, D], BF16)
        x3_dbg = outp("x3_dbg", [TH, D])

    SC2 = float(64.0 ** (-0.5))

    p = Prog()
    with ExitStack() as CTX:
        def sb(name, shape, dt): return CTX.enter_context(nc.sbuf_tensor(name, shape, dt)).ap()
        idb = sb("idb_s", [128, 128], BF16)
        idf = sb("idf_s", [128, 128], F32)
        ones_f = sb("ones_f", [1, 128], F32)
        p.g(lambda g: g.dma_start(out=idb, in_=idb_in), writes=["idb"], dma=True, tag="ldidb")
        p.g(lambda g: g.dma_start(out=idf, in_=idf_in), writes=["idf"], dma=True, tag="ldidf")
        p.g(lambda g: g.memset(ones_f, 1.0), writes=["ones_f"], tag="ones")
        const_sb = sb("const_sb", [128, len(CONST_VALS)], F32)
        p.g(lambda g: g.dma_start(out=const_sb, in_=consts_in), writes=["consts"], dma=True, tag="ldconsts")
        for _i, _v in enumerate(CONST_VALS):
            nc.const_aps.register(F32, _v, const_sb[:, _i:_i + 1])
        x_tm = sb("x_tm", [128, NTC * D], F32)
        p.sy(lambda s: s.dma_start(
            out=x_tm.rearrange("p (tc d) -> p tc d", d=D),
            in_=x2_in.rearrange("(tc p) d -> p tc d", p=128)),
            writes=[("x", tc) for tc in range(NTC)], dma=True, tag="ldx2")

        C12 = CTX.enter_context(ExitStack())
        out_n = C12.enter_context(nc.sbuf_tensor("out_n", [128, NTC * D], BF16)).ap()
        p.g(lambda g: g.memset(out_n, 0.0),
            writes=[("on", tcg, hh) for tcg in range(NTC) for hh in range(16)], tag="on0")
        if "q1" in phases:
          with ExitStack() as C1:
              def sb1(name, shape, dt): return C1.enter_context(nc.sbuf_tensor(name, shape, dt)).ap()
              def ps1_(name, shape, dt=F32): return C1.enter_context(nc.psum_tensor(name, shape, dt)).ap()
              xd_fm = sb1("xd_fm", [128, 8 * T], BF16)
              ps_pj = ps1_("ps_pj", [64, 512])
              ps_vt = ps1_("ps_vt", [128, 64])
              ps_bd = ps1_("ps_bd", [1, 512])
              ps_st = [ps1_("ps_st0", [128, 512]), ps1_("ps_st1", [128, 512])]
              ps_pv = ps1_("ps_pv", [65, 512])
              ps_tr1 = ps1_("ps_tr1", [128, 128], BF16)
              ps_rc = ps1_("ps_rc", [128, 1])
              with nc.sbuf_tensor("xd_tm", [128, NTC * D], BF16) as xd_tm_:
                  xd_tm = xd_tm_.ap()
                  for hf in range(2):
                      p.g(lambda g, hf=hf: g.dma_start(
                          out=xd_tm.rearrange("p (tc d) -> p tc d", d=D),
                          in_=xd_in[hf * TH:(hf + 1) * TH, :].rearrange("(tc p) d -> p tc d", p=128)),
                          writes=["xd_tm"], dma=True, tag=f"ldxd{hf}")
                      for tl in range(NTC):
                          tc = hf * NTC + tl
                          for dc in range(8):
                              p.t(lambda t, tl=tl, dc=dc, xd_tm=xd_tm: t.transpose(
                                  ps_tr1[:, :], xd_tm[:, tl * D + dc * 128: tl * D + (dc + 1) * 128], idb),
                                  reads=["xd_tm", "idb"], writes=["ps_tr1"], tag=f"q0t{tc}{dc}")
                              p.v(lambda v, tc=tc, dc=dc: v.tensor_copy(
                                  xd_fm[:, dc * T + tc * 128: dc * T + (tc + 1) * 128], ps_tr1[:, :]),
                                  reads=["ps_tr1"], writes=[("xdfm", dc, tc)], tag=f"q0te{tc}{dc}")
              p.set_fence()
              KT = sb1("KT", [65, T], BF16)
              QT = sb1("QT", [65, TH], BF16)
              absQ = sb1("absQ", [64, 512], BF16)
              Vau = sb1("Vau", [128, NCH * 65], BF16)
              Mi = sb1("Mi", [64, 1], BF16)
              tkb = sb1("tkb", [128, 64], BF16)
              tqb = sb1("tqb", [128, 64], BF16)
              tvb = sb1("tvb", [128, 64], BF16)
              P_sb = sb1("P_sb", [128, 16 * 512], BF16)
              rcol = sb1("rcol", [128, 1], F32)
              rs_sb = sb1("rs_sb", [1, 512], F32)
              of_h = sb1("of_h", [64, 512], BF16)
              p.g(lambda g: g.memset(Vau, 1.0), writes=["Vau_ones"], tag="vau1")
              p.g(lambda g: g.memset(KT[64:65, :], 1.0), writes=["KTones"], tag="kt1")

              for h in range(16):
                  dch = h // 2
                  pr = (h % 2) * 64
                  p.g(lambda g, h=h, pr=pr: g.dma_start(out=tkb[pr:pr + 64, :], in_=tk_in[h, :, :]),
                      writes=["tkb"], dma=True, tag=f"ltk{h}")
                  p.g(lambda g, h=h, pr=pr: g.dma_start(out=tqb[pr:pr + 64, :], in_=tq_in[h, :, :]),
                      writes=["tqb"], dma=True, tag=f"ltq{h}")
                  p.g(lambda g, h=h, pr=pr: g.dma_start(out=tvb[pr:pr + 64, :], in_=tv_in[h, :, :]),
                      writes=["tvb"], dma=True, tag=f"ltv{h}")
                  for tg in range(8):
                      p.t(lambda t, tg=tg, dch=dch, pr=pr: t.matmul(
                          ps_pj[:, :], tkb[pr:pr + 64, :],
                          xd_fm[pr:pr + 64, dch * T + tg * 512: dch * T + (tg + 1) * 512],
                          start=True, stop=True),
                          reads=["tkb"] + [("xdfm", dch, tg * 4 + i) for i in range(4)],
                          writes=["ps_pj"], tag=f"kp{h}{tg}")
                      p.s(lambda s, tg=tg: s.activation(
                          KT[0:64, tg * 512:(tg + 1) * 512], ps_pj[:, :], AF.Copy, scale=SC2),
                          reads=["ps_pj", "consts"], writes=[("KTh", tg)], tag=f"kpe{h}{tg}")
                  for tg in range(4):
                      p.t(lambda t, tg=tg, dch=dch, pr=pr: t.matmul(
                          ps_pj[:, :], tqb[pr:pr + 64, :],
                          xd_fm[pr:pr + 64, dch * T + tg * 512: dch * T + (tg + 1) * 512],
                          start=True, stop=True),
                          reads=["tqb"] + [("xdfm", dch, tg * 4 + i) for i in range(4)],
                          writes=["ps_pj"], tag=f"qp{h}{tg}")
                      p.s(lambda s, tg=tg: s.activation(
                          QT[0:64, tg * 512:(tg + 1) * 512], ps_pj[:, :], AF.Copy),
                          reads=["ps_pj"], writes=[("QTh", tg)], tag=f"qpe{h}{tg}")

                  for tc in range(NCH):
                      p.t(lambda t, tc=tc, dch=dch, pr=pr: t.matmul(
                          ps_vt[:, :],
                          xd_fm[pr:pr + 64, dch * T + tc * 128: dch * T + (tc + 1) * 128],
                          tvb[pr:pr + 64, :], start=True, stop=True),
                          reads=["tvb", ("xdfm", dch, tc)], writes=["ps_vt"], tag=f"vp{h}{tc}")
                      p.v(lambda v, tc=tc: v.tensor_copy(
                          Vau[:, tc * 65: tc * 65 + 64], ps_vt[:, :]),
                          reads=["ps_vt"], writes=[("Vau", tc)], tag=f"vpe{h}{tc}")
                  p.v(lambda v: v.reduce_max(Mi[:, :], KT[0:64, :], axis=AX.X, apply_absolute_value=True),
                      reads=[("KTh", tg) for tg in range(8)], writes=["Mi"], tag=f"mi{h}")
                  for tg in range(4):
                      p.s(lambda s, tg=tg: s.activation(
                          absQ[:, :], QT[0:64, tg * 512:(tg + 1) * 512], AF.Abs),
                          reads=[("QTh", tg)], writes=["absQ"], tag=f"qab{h}{tg}")
                      p.t(lambda t, tg=tg: t.matmul(
                          ps_bd[:, :], Mi[:, :], absQ[:, :], start=True, stop=True),
                          reads=["Mi", "absQ"], writes=["ps_bd"], tag=f"bd{h}{tg}")
                      p.v(lambda v, tg=tg: v.tensor_scalar(
                          QT[64:65, tg * 512:(tg + 1) * 512], ps_bd[:, :], -1.0, None, op0=ALU.mult),
                          reads=["ps_bd"], writes=[("QTc", tg)], tag=f"bde{h}{tg}")
                  for qg in range(4):
                      for kh in range(2):
                          for kt_loc in range(16):
                              kt = kh * 16 + kt_loc
                              pp = kt % 2
                              p.t(lambda t, kt=kt, qg=qg, pp=pp: t.matmul(
                                  ps_st[pp][:, :], KT[:, kt * 128:(kt + 1) * 128],
                                  QT[:, qg * 512:(qg + 1) * 512], start=True, stop=True),
                                  reads=[("KTh", kt // 4), "KTones", ("QTh", qg), ("QTc", qg)],
                                  writes=[("ps_st", pp)], tag=f"st{h}{qg}{kt}")
                              p.s(lambda s, kt_loc=kt_loc, pp=pp: s.activation(
                                  P_sb[:, kt_loc * 512:(kt_loc + 1) * 512], ps_st[pp][:, :], AF.Exp),
                                  reads=[("ps_st", pp)], writes=[("P", kt_loc)], tag=f"ex{h}{qg}{kt}")
                          for kt_loc in range(16):
                              kt = kh * 16 + kt_loc
                              p.t(lambda t, kt=kt, kt_loc=kt_loc: t.matmul(
                                  ps_pv[:, :], Vau[:, kt * 65:(kt + 1) * 65],
                                  P_sb[:, kt_loc * 512:(kt_loc + 1) * 512],
                                  start=(kt == 0), stop=(kt == NCH - 1)),
                                  reads=[("Vau", kt), "Vau_ones", ("P", kt_loc)],
                                  writes=["ps_pv"], tag=f"pv{h}{qg}{kt}")
                      p.v(lambda v: v.tensor_copy(of_h[:, :], ps_pv[0:64, :]),
                          reads=["ps_pv"], writes=["of_h"], tag=f"pve{h}{qg}")
                      p.v(lambda v: v.tensor_copy(rs_sb[:, :], ps_pv[64:65, :]),
                          reads=["ps_pv"], writes=["rs_sb"], tag=f"rse{h}{qg}")
                      for tc4 in range(4):
                          p.t(lambda t, tc4=tc4: t.transpose(
                              ps_tr1[0:128, 0:64], of_h[:, tc4 * 128:(tc4 + 1) * 128], idb[0:64, 0:64]),
                              reads=["of_h", "idb"], writes=["ps_tr1"], tag=f"ot{h}{qg}{tc4}")
                          p.t(lambda t, tc4=tc4: t.transpose(
                              ps_rc[:, :], rs_sb[:, tc4 * 128:(tc4 + 1) * 128], idf[0:1, 0:1]),
                              reads=["rs_sb", "idf"], writes=["ps_rc"], tag=f"rt{h}{qg}{tc4}")
                          p.v(lambda v: v.reciprocal(rcol[:, :], ps_rc[:, :]),
                              reads=["ps_rc"], writes=["rcol"], tag=f"rc{h}{qg}{tc4}")
                          tcg = qg * 4 + tc4
                          p.v(lambda v, tcg=tcg, h=h: v.tensor_scalar(
                              out_n[:, tcg * D + h * 64: tcg * D + (h + 1) * 64],
                              ps_tr1[0:128, 0:64], rcol[:, :], None, op0=ALU.mult),
                              reads=["ps_tr1", "rcol"], writes=[("on", tcg, h)], tag=f"one{h}{qg}{tc4}")
        p.set_fence()
        if dbg:
            p.dbg_toks = []
            for tc in range(NTC):
                p.dbg_toks.append(p.sy(lambda s, tc=tc: s.dma_start(
                    out=on_dbg[tc * 128:(tc + 1) * 128, :], in_=out_n[:, tc * D:(tc + 1) * D]),
                    reads=[("on", tc, hh) for hh in range(16)], writes=[("ond", tc)],
                    dma=True, tag=f"ond{tc}"))

        if "q2" in phases:
          with ExitStack() as C2:
              def sb2(name, shape, dt): return C2.enter_context(nc.sbuf_tensor(name, shape, dt)).ap()
              def ps2_(name, shape, dt=F32): return C2.enter_context(nc.psum_tensor(name, shape, dt)).ap()
              on_fm = sb2("on_fm", [128, 8 * TH], BF16)
              Wu_bf = sb2("Wu_bf", [128, 8 * D], BF16)
              bu_bc = sb2("bu_bc", [128, D], F32)
              burow = sb2("burow", [1, D], F32)
              t1b = sb2("q2t1", [128, D], F32)
              ps_y = ps2_("ps_y2", [128, D])
              ps_tr2 = ps2_("ps_tr2", [128, 128], BF16)
              for dc in range(8):
                  p.g(lambda g, dc=dc: g.dma_start(
                      out=Wu_bf[:, dc * D:(dc + 1) * D], in_=wuni_in[dc * 128:(dc + 1) * 128, :]),
                      writes=[("Wu", dc)], dma=True, tag=f"ldWu{dc}")
              p.g(lambda g: g.dma_start(out=burow, in_=bu_in), writes=["burow"], dma=True, tag="ldbu")
              for half in range(2):
                  p.t(lambda t, half=half: t.matmul(
                      ps_y[:, half * 512:(half + 1) * 512], ones_f,
                      burow[:, half * 512:(half + 1) * 512], start=True, stop=True),
                      reads=["burow", "ones_f"], writes=["ps_y2"], tag=f"bubc{half}")
              p.v(lambda v: v.tensor_copy(bu_bc, ps_y),
                  reads=["ps_y2"], writes=["bu_bc"], tag="bubcev")
              for tc in range(NTC):
                  for dc in range(8):
                      p.t(lambda t, tc=tc, dc=dc: t.transpose(
                          ps_tr2[:, :], out_n[:, tc * D + dc * 128: tc * D + (dc + 1) * 128], idb),
                          reads=[("on", tc, 2 * dc), ("on", tc, 2 * dc + 1), "idb"],
                          writes=["ps_tr2b"], tag=f"q2t{tc}{dc}")
                      p.v(lambda v, tc=tc, dc=dc: v.tensor_copy(
                          on_fm[:, dc * TH + tc * 128: dc * TH + (tc + 1) * 128], ps_tr2[:, :]),
                          reads=["ps_tr2b"], writes=[("onfm", dc, tc)], tag=f"q2te{tc}{dc}")
              for tc in range(NTC):
                  for half in range(2):
                      for dc in range(8):
                          p.t(lambda t, tc=tc, dc=dc, half=half: t.matmul(
                              ps_y[:, half * 512:(half + 1) * 512],
                              on_fm[:, dc * TH + tc * 128: dc * TH + (tc + 1) * 128],
                              Wu_bf[:, dc * D + half * 512: dc * D + half * 512 + 512],
                              start=(dc == 0), stop=(dc == 7)),
                              reads=[("onfm", dc, tc), ("Wu", dc)], writes=["ps_y2"], tag=f"yu{tc}{half}{dc}")
                  p.v(lambda v: v.tensor_tensor(t1b[:, :], ps_y[:, :], bu_bc, op=ALU.add),
                      reads=["ps_y2", "bu_bc"], writes=["q2t1b"], tag=f"q2a{tc}")
                  p.v(lambda v, tc=tc: v.tensor_tensor(
                      x_tm[:, tc * D:(tc + 1) * D], x_tm[:, tc * D:(tc + 1) * D], t1b[:, :], op=ALU.add),
                      reads=["q2t1b", ("x", tc)], writes=[("x", tc)], tag=f"q2r{tc}")
              if dbg:
                  for tc in range(NTC):
                      p.dbg_toks.append(p.sy(lambda s, tc=tc: s.dma_start(
                          out=x3_dbg[tc * 128:(tc + 1) * 128, :], in_=x_tm[:, tc * D:(tc + 1) * D]),
                          reads=[("x", tc)], writes=[("x3d", tc)], dma=True, tag=f"x3d{tc}"))
        C12.close()
        p.set_fence()

        if "q3" in phases:
            emit_moe(p, nc, CTX, "l1f", x_tm, idb, idf, ones_f,
                     dict(kind="relu", wr_in=wr1_in, wg_in=None,
                          w1_in=w11_in, wu_in=None, w2_in=w21_in,
                          br_in=br1_in, b1_in=b11_in, b2_in=b21_in))
        p.set_fence()

        st_toks = []
        for tc in range(NTC):
            st_toks.append(p.sy(lambda s, tc=tc: s.dma_start(
                out=x4_out[tc * 128:(tc + 1) * 128, :], in_=x_tm[:, tc * D:(tc + 1) * D]),
                reads=[("x", tc)], dma=True, tag=f"stx4{tc}"))
        p.sy(lambda s: s.nop(), extra_deps=st_toks + getattr(p, "dbg_toks", []), tag="fin")

        p.finalize(nc, CTX)
        with nc.Block() as block:
            p.emit(block)
    return nc


_CACHE = {}

MASKS = {
    "maskL": np.tril(np.ones((128, 128), np.float32), -1),
    "maskU": np.triu(np.ones((128, 128), np.float32), 0),
    "idf": np.eye(128, dtype=np.float32),
    "idb": np.eye(128, dtype=np.float32).astype(BF),
    "consts": np.tile(np.asarray(CONST_VALS, np.float32)[None, :], (128, 1)),
}


def kernel(**inputs):
    ii = {k: np.asarray(v) for k, v in inputs.items()}
    if "l1" not in _CACHE:
        _CACHE["l1"] = build_launch1()
    nc1 = _CACHE["l1"]
    shared1 = {
        "Wq": ii["L0_attn_Wq"], "Wk": ii["L0_attn_Wk"], "Wv": ii["L0_attn_Wv"],
        "Wb": ii["L0_attn_Wb"],
        "convq": np.ascontiguousarray(ii["L0_attn_conv_q"].T),
        "convk": np.ascontiguousarray(ii["L0_attn_conv_k"].T),
        "convv": np.ascontiguousarray(ii["L0_attn_conv_v"].T),
        "g_rms": np.ascontiguousarray(ii["L0_attn_g_rms"][None, :]),
        "Wo": ii["L0_attn_Wo"],
        "ln_g": np.ascontiguousarray(ii["L0_attn_ln_g"][None, :]),
        "ln_b": np.ascontiguousarray(ii["L0_attn_ln_b"][None, :]),
        "w_router": ii["L0_ffn_w_router"], "w_gating": ii["L0_ffn_w_gating"],
        "w_gate": ii["L0_ffn_w_gate"], "w_up": ii["L0_ffn_w_up"],
        "w_down": ii["L0_ffn_w_down"],
        "W_down1": ii["L1_attn_W_down"],
        **MASKS,
    }
    x = ii["L0_attn_x"]
    in_maps1 = []
    for c in range(8):
        b, s = c // 2, c % 2
        in_maps1.append({
            "x_b": np.ascontiguousarray(x[b]),
            "x_half": np.ascontiguousarray(x[b, s * TH:(s + 1) * TH]),
            "s_sel": np.array([[s]], np.uint32),
            **shared1,
        })
    res1 = run_bass_kernel_spmd(nc1, in_maps1, core_ids=list(range(8)))
    x2 = [np.asarray(res1.results[c]["x2_half"]) for c in range(8)]
    xd = [np.asarray(res1.results[c]["xd_half"]) for c in range(8)]

    if "l2" not in _CACHE:
        _CACHE["l2"] = build_launch2()
    nc2 = _CACHE["l2"]
    shared2 = {
        "tokeysT": np.ascontiguousarray(np.transpose(ii["L1_attn_tokeys"], (0, 2, 1))),
        "toqueriesT": np.ascontiguousarray(np.transpose(ii["L1_attn_toqueries"], (0, 2, 1))),
        "tovaluesT": np.ascontiguousarray(np.transpose(ii["L1_attn_tovalues"], (0, 2, 1))),
        "W_unify": ii["L1_attn_W_unify"],
        "b_unify": np.ascontiguousarray(ii["L1_attn_b_unify"][None, :]),
        "Wr1": ii["L1_ffn_Wr"],
        "br1": np.ascontiguousarray(ii["L1_ffn_br"][None, :]),
        "W11": ii["L1_ffn_W1"],
        "b11": np.ascontiguousarray(ii["L1_ffn_b1"].reshape(E, 32, 128).transpose(0, 2, 1)),
        "W21": ii["L1_ffn_W2"], "b21": ii["L1_ffn_b2"],
        **MASKS,
    }
    in_maps2 = []
    for c in range(8):
        b, s = c // 2, c % 2
        prt = 2 * b + (1 - s)
        xd_perm = np.concatenate([xd[c], xd[prt]], axis=0)
        in_maps2.append({
            "xd_full": np.ascontiguousarray(xd_perm),
            "x2_half": x2[c],
            **shared2,
        })
    res2 = run_bass_kernel_spmd(nc2, in_maps2, core_ids=list(range(8)))
    out = np.empty((4, T, D), np.float32)
    for c in range(8):
        b, s = c // 2, c % 2
        out[b, s * TH:(s + 1) * TH] = np.asarray(res2.results[c]["x4_half"])
    return out


# version 02
# speedup vs baseline: 1.0045x; 1.0045x over previous
import numpy as np
import ml_dtypes
from contextlib import ExitStack

import concourse.bass as bass
import concourse.mybir as mybir
from concourse.bass_utils import run_bass_kernel_spmd

F32 = mybir.dt.float32
BF16 = mybir.dt.bfloat16
U32 = mybir.dt.uint32
AF = mybir.ActivationFunctionType
ALU = mybir.AluOpType
AX = mybir.AxisListType
BF = ml_dtypes.bfloat16
ds = bass.ds

T = 4096
TH = 2048
D = 1024
H4 = 4
DH = 256
NG = 8
NCH = 32
NTC = 16
E = 8
FF = 4096
NFC = 32
NLEV = 7
CONST_VALS = [1e-5, 1e-6, 1.0 / 1024.0, 1.0 / 256.0, 10.0, float(64.0 ** (-0.5))]

ENGINES = ("tensor", "vector", "scalar", "gpsimd", "sync")
EIDX = {e: i for i, e in enumerate(ENGINES)}


class _Op:
    __slots__ = ("fn", "deps", "dma", "tag", "needs_inc", "inc_after")

    def __init__(self, fn, deps, dma, tag):
        self.fn = fn
        self.deps = deps
        self.dma = dma
        self.tag = tag
        self.needs_inc = False
        self.inc_after = 0


SERIAL = False
MOE_CW_ONE = False


class Prog:
    def __init__(self):
        self.ops = {e: [] for e in ENGINES}
        self.last_write = {}
        self.readers = {}
        self.fence_toks = []
        self._prev = None

    def set_fence(self):
        self.fence_toks = [(e, len(self.ops[e]) - 1) for e in ENGINES if self.ops[e]]

    def op(self, eng, fn, reads=(), writes=(), dma=False, tag="", extra_deps=()):
        deps = set()
        for r in reads:
            t = self.last_write.get(r)
            if t is not None:
                deps.add(t)
        for w in writes:
            t = self.last_write.get(w)
            if t is not None:
                deps.add(t)
            for t2 in self.readers.get(w, ()):
                deps.add(t2)
        for t in extra_deps:
            if t is not None:
                deps.add(t)
        for t in self.fence_toks:
            deps.add(t)
        if eng == "tensor":
            deps = {d for d in deps if d[0] != "tensor"}
        if SERIAL and self._prev is not None and self._prev != (eng, len(self.ops[eng])):
            deps.add(self._prev)
        deps.discard((eng, len(self.ops[eng])))
        tok = (eng, len(self.ops[eng]))
        self.ops[eng].append(_Op(fn, sorted(deps), dma, tag))
        self._prev = tok
        for r in reads:
            self.readers.setdefault(r, []).append(tok)
        for w in writes:
            self.last_write[w] = tok
            self.readers[w] = []
        return tok

    def t(self, fn, **kw): return self.op("tensor", fn, **kw)
    def v(self, fn, **kw): return self.op("vector", fn, **kw)
    def s(self, fn, **kw): return self.op("scalar", fn, **kw)
    def g(self, fn, **kw): return self.op("gpsimd", fn, **kw)
    def sy(self, fn, **kw): return self.op("sync", fn, **kw)

    def finalize(self, nc, ctx):
        for e in ENGINES:
            for o in self.ops[e]:
                if o.dma:
                    o.needs_inc = True
                for (pe, pi) in o.deps:
                    self.ops[pe][pi].needs_inc = True
        semkeys = set()
        for e in ENGINES:
            cnt = {}
            for o in self.ops[e]:
                k = e + ("D" if o.dma else "")
                if o.needs_inc:
                    cnt[k] = cnt.get(k, 0) + (16 if o.dma else 1)
                    semkeys.add(k)
                o.inc_after = cnt.get(k, 0)
        self.sems = {k: ctx.enter_context(nc.semaphore(f"sem_{k}")) for k in semkeys}
        self.check()

    def check(self):
        done = {e: 0 for e in ENGINES}
        total = sum(len(self.ops[e]) for e in ENGINES)
        ndone = 0
        progress = True
        while progress:
            progress = False
            for e in ENGINES:
                while done[e] < len(self.ops[e]):
                    o = self.ops[e][done[e]]
                    if not all(pi < done[pe] for (pe, pi) in o.deps):
                        break
                    done[e] += 1
                    ndone += 1
                    progress = True
        if ndone != total:
            for e in ENGINES:
                if done[e] < len(self.ops[e]):
                    o = self.ops[e][done[e]]
                    raise RuntimeError(
                        f"deadlock at {e}[{done[e]}] tag={o.tag} deps={o.deps}")

    def emit(self, block):
        op_know = {}
        SEMKEYS = [e for e in ENGINES] + [e + "D" for e in ENGINES]
        KIDX = {k: i for i, k in enumerate(SEMKEYS)}
        know = {e: [0] * len(SEMKEYS) for e in ENGINES}

        def mk_body(eng):
            def body(engine_obj):
                kn = know[eng]
                for i, o in enumerate(self.ops[eng]):
                    for (pe, pi) in o.deps:
                        po = self.ops[pe][pi]
                        pk_key = pe + ("D" if po.dma else "")
                        need = po.inc_after
                        j = KIDX[pk_key]
                        if need > kn[j]:
                            engine_obj.wait_ge(self.sems[pk_key], need)
                            kn[j] = need
                            pk = op_know.get((pe, pi))
                            if pk:
                                for jj in range(len(SEMKEYS)):
                                    if pk[jj] > kn[jj]:
                                        kn[jj] = pk[jj]
                    inst = o.fn(engine_obj)
                    if o.needs_inc:
                        if inst is None:
                            raise RuntimeError(f"{eng}[{i}] {o.tag}: inc needed, no inst")
                        inst.then_inc(self.sems[eng + ("D" if o.dma else "")], 16 if o.dma else 1)
                    op_know[(eng, i)] = list(kn)
            return body

        for eng in ENGINES:
            if self.ops[eng]:
                getattr(block, eng)(mk_body(eng))



def mk_ln_scratch(nc, ctx, name, w=1024):
    S = {"_n": name}
    for k, shape in (("s1", [128, 1]), ("nmu", [128, 1]), ("xc", [128, w]),
                     ("xsq", [128, w]), ("ssq", [128, 1]), ("sv", [128, 1]),
                     ("rstd", [128, 1])):
        S[k] = ctx.enter_context(nc.sbuf_tensor(name + "_" + k, shape, F32)).ap()[:, :]
    return S


def emit_ln_chunk(p, x_ap, x_key, scr, out_ap, out_keys, pref, eps=1e-5, n=1024):
    inv_n = 1.0 / n
    nm = scr["_n"]
    p.v(lambda v: v.reduce_sum(scr["s1"], x_ap, axis=AX.X),
        reads=[x_key], writes=[nm + "s1"], tag=pref + "sum")
    p.v(lambda v: v.tensor_scalar(scr["nmu"], scr["s1"], -inv_n, None, op0=ALU.mult),
        reads=[nm + "s1"], writes=[nm + "nmu"], tag=pref + "nmu")
    p.v(lambda v: v.tensor_scalar(scr["xc"], x_ap, scr["nmu"], None, op0=ALU.add),
        reads=[x_key, nm + "nmu"], writes=[nm + "xc"], tag=pref + "xc")
    p.s(lambda s: s.activation(scr["xsq"], scr["xc"], AF.Square, accum_out=scr["ssq"]),
        reads=[nm + "xc"], writes=[nm + "xsq", nm + "ssq"], tag=pref + "sq")
    p.s(lambda s: s.activation(scr["sv"], scr["ssq"], AF.Sqrt, bias=float(eps), scale=inv_n),
        reads=[nm + "ssq", "consts"], writes=[nm + "sv"], tag=pref + "sv")
    p.v(lambda v: v.reciprocal(scr["rstd"], scr["sv"]),
        reads=[nm + "sv"], writes=[nm + "rstd"], tag=pref + "rstd")
    p.v(lambda v: v.tensor_scalar(out_ap, scr["xc"], scr["rstd"], None, op0=ALU.mult),
        reads=[nm + "xc", nm + "rstd"], writes=list(out_keys), tag=pref + "out")
    return nm + "xc", nm + "rstd"


def emit_moe(p, nc, CTX, pref, x_tm, idb, idf, ones_f, cfg):
    kind = cfg["kind"]
    with ExitStack() as C:
        def sbX(name, shape, dt): return C.enter_context(nc.sbuf_tensor(pref + name, shape, dt)).ap()
        def psX(name, shape, dt=F32): return C.enter_context(nc.psum_tensor(pref + name, shape, dt)).ap()
        B = [psX(f"_B{i}", [128, 512]) for i in range(6)]
        Btr = psX("_Btr", [128, 128], BF16)
        Bl = psX("_Bl", [128, 16], F32)
        xln_fm = sbX("_xfm", [128, 8 * TH], BF16)
        xln_c = sbX("_xc", [128, D], BF16)
        wbuf = sbX("_w", [128, NTC * 8], F32)
        p.moe_wbuf = wbuf
        lnscX = mk_ln_scratch(nc, C, pref + "_ln")
        for tc in range(NTC):
            emit_ln_chunk(p, x_tm[:, tc * D:(tc + 1) * D], ("x", tc), lnscX,
                          xln_c[:, :], [pref + "xlnc"], f"{pref}l{tc}")
            for dc in range(8):
                p.t(lambda t, dc=dc: t.transpose(
                    Btr[:, :], xln_c[:, dc * 128:(dc + 1) * 128], idb),
                    reads=[pref + "xlnc", "idb"], writes=[pref + "Btr"], tag=f"{pref}t{tc}{dc}")
                p.v(lambda v, tc=tc, dc=dc: v.tensor_copy(
                    xln_fm[:, dc * TH + tc * 128: dc * TH + (tc + 1) * 128], Btr[:, :]),
                    reads=[pref + "Btr"], writes=[(pref + "fm", dc, tc)], tag=f"{pref}te{tc}{dc}")
        if kind == "glu":
            WrWg = sbX("_wrg", [128, 8 * 16], BF16)
            for dc in range(8):
                p.g(lambda g, dc=dc: g.dma_start(
                    out=WrWg[:, dc * 16: dc * 16 + 8], in_=cfg["wr_in"][dc * 128:(dc + 1) * 128, :]),
                    writes=[(pref + "wr", dc)], dma=True, tag=f"{pref}lwr{dc}")
                p.g(lambda g, dc=dc: g.dma_start(
                    out=WrWg[:, dc * 16 + 8: dc * 16 + 16], in_=cfg["wg_in"][dc * 128:(dc + 1) * 128, :]),
                    writes=[(pref + "wr", dc)], dma=True, tag=f"{pref}lwg{dc}")
            sc8a = sbX("_s8a", [128, 8], F32)
            sc8b = sbX("_s8b", [128, 8], F32)
            sc8c = sbX("_s8c", [128, 8], F32)
            col1 = sbX("_c1", [128, 1], F32)
            col1b = sbX("_c1b", [128, 1], F32)
            col1c = sbX("_c1c", [128, 1], F32)
            col2 = sbX("_c2", [128, 1], F32)
            col3 = sbX("_c3", [128, 1], F32)
            for tc in range(NTC):
                for dc in range(8):
                    p.t(lambda t, tc=tc, dc=dc: t.matmul(
                        Bl[:, :],
                        xln_fm[:, dc * TH + tc * 128: dc * TH + (tc + 1) * 128],
                        WrWg[:, dc * 16:(dc + 1) * 16],
                        start=(dc == 0), stop=(dc == 7)),
                        reads=[(pref + "fm", dc, tc), (pref + "wr", dc)],
                        writes=[pref + "Bl"], tag=f"{pref}rl{tc}{dc}")
                q = f"{pref}r{tc}"
                p.v(lambda v: v.reduce_max(col1[:, :], Bl[:, 0:8], axis=AX.X, negate=True),
                    reads=[pref + "Bl"], writes=[pref + "nmr"], tag=q + "mr")
                p.s(lambda s: s.activation(sc8a[:, :], Bl[:, 0:8], AF.Exp, bias=col1[:, :],
                                           accum_out=col2[:, :]),
                    reads=[pref + "Bl", pref + "nmr"], writes=[pref + "re", pref + "rs"], tag=q + "ex")
                p.v(lambda v: v.reciprocal(col2[:, :], col2[:, :]),
                    reads=[pref + "rs"], writes=[pref + "rs"], tag=q + "rr")
                p.v(lambda v: v.tensor_scalar(sc8a[:, :], sc8a[:, :], col2[:, :], None, op0=ALU.mult),
                    reads=[pref + "re", pref + "rs"], writes=[pref + "re"], tag=q + "rp")
                p.v(lambda v: v.reduce_max(col1b[:, :], Bl[:, 8:16], axis=AX.X, negate=True),
                    reads=[pref + "Bl"], writes=[pref + "nmg"], tag=q + "mg")
                p.v(lambda v: v.tensor_scalar(col1b[:, :], col1b[:, :], 10.0, None, op0=ALU.mult),
                    reads=[pref + "nmg"], writes=[pref + "nmg"], tag=q + "mg10")
                p.s(lambda s: s.activation(sc8b[:, :], Bl[:, 8:16], AF.Exp, bias=col1b[:, :],
                                           scale=10.0, accum_out=col3[:, :]),
                    reads=[pref + "Bl", pref + "nmg", "consts"], writes=[pref + "ge", pref + "gs"], tag=q + "gex")
                p.v(lambda v: v.reciprocal(col3[:, :], col3[:, :]),
                    reads=[pref + "gs"], writes=[pref + "gs"], tag=q + "gr")
                p.v(lambda v: v.tensor_scalar(sc8b[:, :], sc8b[:, :], col3[:, :], None, op0=ALU.mult),
                    reads=[pref + "ge", pref + "gs"], writes=[pref + "ge"], tag=q + "gp")
                p.v(lambda v: v.tensor_tensor(sc8c[:, :], sc8a[:, :], sc8b[:, :], op=ALU.mult),
                    reads=[pref + "re", pref + "ge"], writes=[pref + "c0"], tag=q + "cw0")
                p.v(lambda v: v.reduce_sum(col1c[:, :], sc8c[:, :], axis=AX.X),
                    reads=[pref + "c0"], writes=[pref + "cs"], tag=q + "csum")
                p.v(lambda v: v.tensor_scalar(col1c[:, :], col1c[:, :], 1e-9, None, op0=ALU.add),
                    reads=[pref + "cs"], writes=[pref + "cs"], tag=q + "cse")
                p.v(lambda v: v.reciprocal(col1c[:, :], col1c[:, :]),
                    reads=[pref + "cs"], writes=[pref + "cs"], tag=q + "csr")
                p.v(lambda v, tc=tc: v.tensor_scalar(
                    wbuf[:, tc * 8:(tc + 1) * 8], sc8c[:, :], col1c[:, :], None, op0=ALU.mult),
                    reads=[pref + "c0", pref + "cs"], writes=[(pref + "w", tc)], tag=q + "cw")
        else:
            Wr_bf = sbX("_wrb", [128, 8 * 8], BF16)
            br_bc = sbX("_brbc", [128, 8], F32)
            brrow = sbX("_brrow", [1, 8], F32)
            for dc in range(8):
                p.g(lambda g, dc=dc: g.dma_start(
                    out=Wr_bf[:, dc * 8:(dc + 1) * 8], in_=cfg["wr_in"][dc * 128:(dc + 1) * 128, :]),
                    writes=[(pref + "wr", dc)], dma=True, tag=f"{pref}lwr{dc}")
            p.g(lambda g: g.dma_start(out=brrow, in_=cfg["br_in"]),
                writes=[pref + "brrow"], dma=True, tag=f"{pref}lbr")
            p.t(lambda t: t.matmul(Bl[:, 0:8], ones_f, brrow, start=True, stop=True),
                reads=[pref + "brrow", "ones_f"], writes=[pref + "Bl"], tag=f"{pref}brbc")
            p.v(lambda v: v.tensor_copy(br_bc, Bl[:, 0:8]),
                reads=[pref + "Bl"], writes=[pref + "brbc"], tag=f"{pref}brev")
            lg8 = sbX("_lg8", [128, 8], F32)
            ex8 = sbX("_ex8", [128, 8], F32)
            mk8 = sbX("_mk8", [128, 8], F32)
            colm = sbX("_colm", [128, 1], F32)
            coln = sbX("_coln", [128, 1], F32)
            cols = sbX("_cols", [128, 1], F32)
            for tc in range(NTC):
                for dc in range(8):
                    p.t(lambda t, tc=tc, dc=dc: t.matmul(
                        Bl[:, 0:8],
                        xln_fm[:, dc * TH + tc * 128: dc * TH + (tc + 1) * 128],
                        Wr_bf[:, dc * 8:(dc + 1) * 8],
                        start=(dc == 0), stop=(dc == 7)),
                        reads=[(pref + "fm", dc, tc), (pref + "wr", dc)],
                        writes=[pref + "Bl"], tag=f"{pref}rl{tc}{dc}")
                q = f"{pref}r{tc}"
                p.v(lambda v: v.tensor_tensor(lg8[:, :], Bl[:, 0:8], br_bc, op=ALU.add),
                    reads=[pref + "Bl", pref + "brbc"], writes=[pref + "lg"], tag=q + "lg")
                p.v(lambda v: v.reduce_max(colm[:, :], lg8[:, :], axis=AX.X),
                    reads=[pref + "lg"], writes=[pref + "m"], tag=q + "m")
                p.v(lambda v: v.tensor_scalar(coln[:, :], colm[:, :], -1.0, None, op0=ALU.mult),
                    reads=[pref + "m"], writes=[pref + "nm"], tag=q + "nm")
                p.s(lambda s: s.activation(ex8[:, :], lg8[:, :], AF.Exp, bias=coln[:, :],
                                           accum_out=cols[:, :]),
                    reads=[pref + "lg", pref + "nm"], writes=[pref + "ex", pref + "es"], tag=q + "ex")
                p.v(lambda v: v.reciprocal(cols[:, :], cols[:, :]),
                    reads=[pref + "es"], writes=[pref + "es"], tag=q + "gt")
                p.v(lambda v: v.tensor_scalar(mk8[:, :], lg8[:, :], colm[:, :], None, op0=ALU.is_ge),
                    reads=[pref + "lg", pref + "m"], writes=[pref + "msk"], tag=q + "msk")
                p.v(lambda v, tc=tc: v.tensor_scalar(
                    wbuf[:, tc * 8:(tc + 1) * 8], mk8[:, :], cols[:, :], None, op0=ALU.mult),
                    reads=[pref + "msk", pref + "es"], writes=[(pref + "w", tc)], tag=q + "w")

        h_sb = [sbX("_h0", [128, NFC * 512], BF16), sbX("_h1", [128, NFC * 512], BF16)]
        w1sl = sbX("_w1s", [128, 2 * 1024], BF16)
        wusl = sbX("_wus", [128, 2 * 1024], BF16) if kind == "glu" else None
        w2sl = sbX("_w2s", [128, 2 * 512], BF16)
        gs_sc = [sbX("_gs0", [128, 512], F32), sbX("_gs1", [128, 512], F32)]
        b1_sb = sbX("_b1", [128, NFC], F32) if kind == "relu" else None
        for e in range(E):
            if kind == "relu":
                p.g(lambda g, e=e: g.dma_start(out=b1_sb, in_=cfg["b1_in"][e, :, :]),
                    writes=[pref + "b1"], dma=True, tag=f"{pref}lb1{e}")
            for tsp in range(2):
                for fc in range(NFC):
                    par = fc % 2
                    w1ap = w1sl[:, par * 1024:(par + 1) * 1024]
                    p.g(lambda g, e=e, fc=fc, w1ap=w1ap: g.dma_start(
                        out=w1ap.rearrange("p (dc f) -> p dc f", f=128),
                        in_=cfg["w1_in"][e, :, fc * 128:(fc + 1) * 128]
                        .rearrange("(dc p) f -> p dc f", p=128)),
                        writes=[(pref + "w1s", par)], dma=True, tag=f"{pref}lw1{e}{tsp}{fc}")
                    if kind == "glu":
                        wuap = wusl[:, par * 1024:(par + 1) * 1024]
                        p.g(lambda g, e=e, fc=fc, wuap=wuap: g.dma_start(
                            out=wuap.rearrange("p (dc f) -> p dc f", f=128),
                            in_=cfg["wu_in"][e, :, fc * 128:(fc + 1) * 128]
                            .rearrange("(dc p) f -> p dc f", p=128)),
                            writes=[(pref + "wus", par)], dma=True, tag=f"{pref}lwu{e}{tsp}{fc}")
                    for ts_loc in range(2):
                        ts = tsp * 2 + ts_loc
                        for dc in range(8):
                            p.t(lambda t, dc=dc, ts=ts, ts_loc=ts_loc, w1ap=w1ap: t.matmul(
                                B[ts_loc][:, :],
                                w1ap[:, dc * 128:(dc + 1) * 128],
                                xln_fm[:, dc * TH + ts * 512: dc * TH + (ts + 1) * 512],
                                start=(dc == 0), stop=(dc == 7), skip_group_check=True),
                                reads=[(pref + "w1s", par)] + [(pref + "fm", dc, ts * 4 + i) for i in range(4)],
                                writes=[(pref + "B", ts_loc)], tag=f"{pref}g{e}{fc}{ts_loc}{dc}")
                        if kind == "glu":
                            for dc in range(8):
                                p.t(lambda t, dc=dc, ts=ts, ts_loc=ts_loc, wuap=wuap: t.matmul(
                                    B[2 + ts_loc][:, :],
                                    wuap[:, dc * 128:(dc + 1) * 128],
                                    xln_fm[:, dc * TH + ts * 512: dc * TH + (ts + 1) * 512],
                                    start=(dc == 0), stop=(dc == 7), skip_group_check=True),
                                    reads=[(pref + "wus", par)] + [(pref + "fm", dc, ts * 4 + i) for i in range(4)],
                                    writes=[(pref + "B", 2 + ts_loc)], tag=f"{pref}u{e}{fc}{ts_loc}{dc}")
                    for ts_loc in range(2):
                        if kind == "glu":
                            p.s(lambda s, ts_loc=ts_loc: s.activation(
                                gs_sc[ts_loc][:, :], B[ts_loc][:, :], AF.Silu),
                                reads=[(pref + "B", ts_loc)], writes=[(pref + "gs", ts_loc)],
                                tag=f"{pref}sl{e}{tsp}{fc}{ts_loc}")
                            p.v(lambda v, ts_loc=ts_loc, fc=fc: v.tensor_tensor(
                                h_sb[ts_loc][:, fc * 512:(fc + 1) * 512],
                                gs_sc[ts_loc][:, :], B[2 + ts_loc][:, :], op=ALU.mult),
                                reads=[(pref + "gs", ts_loc), (pref + "B", 2 + ts_loc)],
                                writes=[(pref + "h", ts_loc, fc)], tag=f"{pref}h{e}{tsp}{fc}{ts_loc}")
                        else:
                            p.s(lambda s, ts_loc=ts_loc, fc=fc: s.activation(
                                h_sb[ts_loc][:, fc * 512:(fc + 1) * 512], B[ts_loc][:, :],
                                AF.Relu, bias=b1_sb[:, fc:fc + 1]),
                                reads=[(pref + "B", ts_loc), pref + "b1"],
                                writes=[(pref + "h", ts_loc, fc)],
                                tag=f"{pref}h{e}{tsp}{fc}{ts_loc}")
                for ts_loc in range(2):
                    ts = tsp * 2 + ts_loc
                    for half in range(2):
                        for fc in range(NFC):
                            par = fc % 2
                            w2ap = w2sl[:, par * 512:(par + 1) * 512]
                            p.g(lambda g, e=e, fc=fc, half=half, w2ap=w2ap: g.dma_start(
                                out=w2ap,
                                in_=cfg["w2_in"][e, fc * 128:(fc + 1) * 128, half * 512:(half + 1) * 512]),
                                writes=[(pref + "w2s", par)], dma=True,
                                tag=f"{pref}lw2{e}{ts}{half}{fc}")
                            for tt in range(4):
                                p.t(lambda t, fc=fc, tt=tt, ts_loc=ts_loc, w2ap=w2ap: t.matmul(
                                    B[tt][:, :],
                                    h_sb[ts_loc][:, fc * 512 + tt * 128: fc * 512 + (tt + 1) * 128],
                                    w2ap,
                                    start=(fc == 0), stop=(fc == NFC - 1), skip_group_check=True),
                                    reads=[(pref + "h", ts_loc, fc), (pref + "w2s", par)],
                                    writes=[(pref + "B", tt)], tag=f"{pref}d{e}{ts}{half}{fc}{tt}")
                        for tt in range(4):
                            tcg = ts * 4 + tt
                            wc = 1.0 if MOE_CW_ONE else wbuf[:, tcg * 8 + e: tcg * 8 + e + 1]
                            p.v(lambda v, tt=tt, half=half, tcg=tcg, wc=wc: v.scalar_tensor_tensor(
                                x_tm[:, tcg * D + half * 512: tcg * D + half * 512 + 512],
                                B[tt][:, :], wc,
                                x_tm[:, tcg * D + half * 512: tcg * D + half * 512 + 512],
                                op0=ALU.mult, op1=ALU.add),
                                reads=[(pref + "B", tt), (pref + "w", tcg), ("x", tcg)],
                                writes=[("x", tcg)], tag=f"{pref}f{e}{ts}{tt}{half}")
        if kind == "relu":
            w_fm = sbX("_wfm", [8, NTC * 128], BF16)
            b2_bf = sbX("_b2", [8, D], BF16)
            p.g(lambda g: g.dma_start(out=b2_bf, in_=cfg["b2_in"]),
                writes=[pref + "b2"], dma=True, tag=f"{pref}lb2")
            for tc in range(NTC):
                p.t(lambda t, tc=tc: t.transpose(B[4][0:8, 0:128], wbuf[:, tc * 8:(tc + 1) * 8], idf),
                    reads=[(pref + "w", tc), "idf"], writes=[(pref + "B", 4)], tag=f"{pref}wt{tc}")
                p.v(lambda v, tc=tc: v.tensor_copy(w_fm[:, tc * 128:(tc + 1) * 128], B[4][0:8, 0:128]),
                    reads=[(pref + "B", 4)], writes=[(pref + "wfm", tc)], tag=f"{pref}wte{tc}")
            for tc in range(NTC):
                for half in range(2):
                    p.t(lambda t, tc=tc, half=half: t.matmul(
                        B[5][:, :],
                        w_fm[:, tc * 128:(tc + 1) * 128],
                        b2_bf[:, half * 512:(half + 1) * 512],
                        start=True, stop=True),
                        reads=[(pref + "wfm", tc), pref + "b2"], writes=[(pref + "B", 5)],
                        tag=f"{pref}b2m{tc}{half}")
                    p.v(lambda v, tc=tc, half=half: v.tensor_tensor(
                        x_tm[:, tc * D + half * 512: tc * D + half * 512 + 512],
                        x_tm[:, tc * D + half * 512: tc * D + half * 512 + 512],
                        B[5][:, :], op=ALU.add),
                        reads=[(pref + "B", 5), ("x", tc)], writes=[("x", tc)],
                        tag=f"{pref}b2a{tc}{half}")



def emit_scan_head(p, tl, h, c, bufs):
    (q_fm, k_fm, k_tmc, v_tmc, beta_g, S_all, Sbf_all, Mb, MTb, Pb, PTb,
     t256, cur, D_bf, Wmb, mL, mU, idf, ps_a, ps_b, ps_c, ps_d, ps_trf,
     osq, ssq_c, sv_c, rst_c, grms_bc, o_stage, ostg_key) = bufs
    Qt = [q_fm[:, (2 * h + kh) * 512 + tl * 128: (2 * h + kh) * 512 + (tl + 1) * 128] for kh in range(2)]
    Kt = [k_fm[:, (2 * h + kh) * 512 + tl * 128: (2 * h + kh) * 512 + (tl + 1) * 128] for kh in range(2)]
    Ktm = [k_tmc[:, (2 * h + kh) * 128:(2 * h + kh + 1) * 128] for kh in range(2)]
    Vtm = v_tmc[:, h * 256:(h + 1) * 256]
    Sh = S_all[:, h * 512:(h + 1) * 512]
    Sbh = Sbf_all[:, h * 512:(h + 1) * 512]
    bcol = beta_g[:, tl * 4 + h: tl * 4 + h + 1]
    qk = [("qfm", 2 * h), ("qfm", 2 * h + 1)]
    kk_ = [("kfm", 2 * h), ("kfm", 2 * h + 1)]

    p.t(lambda t: t.matmul(ps_a[:, :], Kt[0], Kt[0], start=True, stop=False),
        reads=[kk_[0]], writes=["ps_a"], tag=f"KK0_{c}_{h}")
    p.t(lambda t: t.matmul(ps_a[:, :], Kt[1], Kt[1], start=False, stop=True),
        reads=[kk_[1]], writes=["ps_a"], tag=f"KK1_{c}_{h}")
    p.v(lambda v: v.scalar_tensor_tensor(Mb[:, :], ps_a[:, :], bcol, mL, op0=ALU.mult, op1=ALU.mult),
        reads=["ps_a", ("beta", tl), "mL"], writes=["Mb"], tag=f"M_{c}_{h}")
    p.t(lambda t: t.transpose(ps_trf[:, :], Mb[:, :], idf),
        reads=["Mb", "idf"], writes=["ps_trf"], tag=f"MT_{c}_{h}")
    p.v(lambda v: v.tensor_copy(MTb[:, :], ps_trf[:, :]),
        reads=["ps_trf"], writes=["MTb"], tag=f"MTe_{c}_{h}")
    p.t(lambda t: t.matmul(ps_c[:, :], Kt[0], Sbh[:, 0:256], start=True, stop=False),
        reads=[kk_[0], ("Sbf", h)], writes=["ps_c"], tag=f"KS0_{c}_{h}")
    p.t(lambda t: t.matmul(ps_c[:, :], Kt[1], Sbh[:, 256:512], start=False, stop=True),
        reads=[kk_[1], ("Sbf", h)], writes=["ps_c"], tag=f"KS1_{c}_{h}")
    p.v(lambda v: v.tensor_tensor(t256[:, :], Vtm, ps_c[:, :], op=ALU.subtract),
        reads=["v_tmc", "ps_c"], writes=["t256"], tag=f"R1_{c}_{h}")
    p.v(lambda v: v.tensor_scalar(cur[0][:, :], t256[:, :], bcol, None, op0=ALU.mult),
        reads=["t256", ("beta", tl)], writes=["cur0b"], tag=f"R2_{c}_{h}")
    p.t(lambda t: t.matmul(ps_d[:, 0:256], MTb[:, :], cur[0][:, :], start=True, stop=True),
        reads=["MTb", "cur0b"], writes=["ps_d"], tag=f"A0_{c}_{h}")
    p.v(lambda v: v.tensor_tensor(cur[1][:, :], cur[0][:, :], ps_d[:, 0:256], op=ALU.subtract),
        reads=["cur0b", "ps_d"], writes=["cur1b"], tag=f"Av0_{c}_{h}")
    cur_k, nxt_k = 1, 0
    Pp, PTp = ("Mb", Mb), ("MTb", MTb)
    pc = 0
    for lev in range(1, NLEV):
        Pc, PTc = (f"P{pc}b", Pb[pc]), (f"PT{pc}b", PTb[pc])
        p.t(lambda t, a=PTp[1], b=Pp[1]: t.matmul(ps_a[:, :], a[:, :], b[:, :], start=True, stop=True),
            reads=[PTp[0], Pp[0]], writes=["ps_a"], tag=f"sq{lev}_{c}_{h}")
        p.v(lambda v, o=Pc[1]: v.tensor_copy(o[:, :], ps_a[:, :]),
            reads=["ps_a"], writes=[Pc[0]], tag=f"sqe{lev}_{c}_{h}")
        p.t(lambda t, a=Pp[1], b=PTp[1]: t.matmul(ps_b[:, :], a[:, :], b[:, :], start=True, stop=True),
            reads=[Pp[0], PTp[0]], writes=["ps_b"], tag=f"sqt{lev}_{c}_{h}")
        p.v(lambda v, o=PTc[1]: v.tensor_copy(o[:, :], ps_b[:, :]),
            reads=["ps_b"], writes=[PTc[0]], tag=f"sqte{lev}_{c}_{h}")
        p.t(lambda t, a=PTc[1], b=cur[cur_k]: t.matmul(ps_d[:, 0:256], a[:, :], b[:, :], start=True, stop=True),
            reads=[PTc[0], f"cur{cur_k}b"], writes=["ps_d"], tag=f"A{lev}_{c}_{h}")
        p.v(lambda v, a=cur[cur_k], o=cur[nxt_k]: v.tensor_tensor(o[:, :], a[:, :], ps_d[:, 0:256], op=ALU.add),
            reads=[f"cur{cur_k}b", "ps_d"], writes=[f"cur{nxt_k}b"], tag=f"Av{lev}_{c}_{h}")
        cur_k, nxt_k = nxt_k, cur_k
        Pp, PTp = Pc, PTc
        pc = 1 - pc
    p.v(lambda v, a=cur[cur_k]: v.tensor_copy(D_bf[:, :], a[:, :]),
        reads=[f"cur{cur_k}b"], writes=["D_bfb"], tag=f"Dbf_{c}_{h}")
    p.t(lambda t: t.matmul(ps_a[:, :], Kt[0], Qt[0], start=True, stop=False),
        reads=[kk_[0], qk[0]], writes=["ps_a"], tag=f"KQ0_{c}_{h}")
    p.t(lambda t: t.matmul(ps_a[:, :], Kt[1], Qt[1], start=False, stop=True),
        reads=[kk_[1], qk[1]], writes=["ps_a"], tag=f"KQ1_{c}_{h}")
    p.v(lambda v: v.tensor_tensor(Wmb[:, :], ps_a[:, :], mU, op=ALU.mult),
        reads=["ps_a", "mU"], writes=["Wmb"], tag=f"Wm_{c}_{h}")
    p.t(lambda t: t.matmul(ps_c[:, :], Qt[0], Sbh[:, 0:256], start=True, stop=False),
        reads=[qk[0], ("Sbf", h)], writes=["ps_c"], tag=f"O0_{c}_{h}")
    p.t(lambda t: t.matmul(ps_c[:, :], Qt[1], Sbh[:, 256:512], start=False, stop=False),
        reads=[qk[1], ("Sbf", h)], writes=["ps_c"], tag=f"O1_{c}_{h}")
    p.t(lambda t: t.matmul(ps_c[:, :], Wmb[:, :], D_bf[:, :], start=False, stop=True),
        reads=["Wmb", "D_bfb"], writes=["ps_c"], tag=f"O2_{c}_{h}")
    p.s(lambda s: s.activation(osq[:, :], ps_c[:, :], AF.Square, accum_out=ssq_c[:, :]),
        reads=["ps_c"], writes=["osq", "ssq_cb"], tag=f"Osq_{c}_{h}")
    p.s(lambda s: s.activation(sv_c[:, :], ssq_c[:, :], AF.Sqrt, bias=1e-5, scale=1.0 / DH),
        reads=["ssq_cb", "consts"], writes=["sv_cb"], tag=f"Osv_{c}_{h}")
    p.v(lambda v: v.reciprocal(rst_c[:, :], sv_c[:, :]),
        reads=["sv_cb"], writes=["rst_cb"], tag=f"Orc_{c}_{h}")
    p.v(lambda v: v.scalar_tensor_tensor(
        o_stage[:, h * 256:(h + 1) * 256], ps_c[:, :], rst_c[:, :], grms_bc,
        op0=ALU.mult, op1=ALU.mult),
        reads=["ps_c", "rst_cb", "grms_bc"], writes=[ostg_key], tag=f"On_{c}_{h}")
    p.t(lambda t: t.matmul(ps_d[:, 0:256], Ktm[0], D_bf[:, :], start=True, stop=True),
        reads=["k_tmc", "D_bfb"], writes=["ps_d"], tag=f"U0_{c}_{h}")
    p.t(lambda t: t.matmul(ps_d[:, 256:512], Ktm[1], D_bf[:, :], start=True, stop=True),
        reads=["k_tmc", "D_bfb"], writes=["ps_d"], tag=f"U1_{c}_{h}")
    p.v(lambda v: v.tensor_tensor(Sh, Sh, ps_d[:, :], op=ALU.add),
        reads=[("S", h), "ps_d"], writes=[("S", h)], tag=f"Su_{c}_{h}")
    p.v(lambda v: v.tensor_copy(Sbh, Sh),
        reads=[("S", h)], writes=[("Sbf", h)], tag=f"Sb_{c}_{h}")



def build_launch1(dbg=False):
    nc = bass.Bass("TRN2", target_bir_lowering=False)
    def inp(name, shape, dt=F32):
        return nc.dram_tensor(name, shape, dt, kind="ExternalInput").ap()
    def outp(name, shape, dt=F32):
        return nc.dram_tensor(name, shape, dt, kind="ExternalOutput").ap()

    x_in = inp("x_b", [T, D])
    xh_in = inp("x_half", [TH, D])
    s_in = inp("s_sel", [1, 1], U32)
    Wq_in = inp("Wq", [D, D]); Wk_in = inp("Wk", [D, D]); Wv_in = inp("Wv", [D, D])
    Wb_in = inp("Wb", [D, 4])
    cq_in = inp("convq", [D, 4]); ck_in = inp("convk", [D, 4]); cv_in = inp("convv", [D, 4])
    grms_in = inp("g_rms", [1, DH])
    Wo_in = inp("Wo", [D, D])
    lng_in = inp("ln_g", [1, D]); lnb_in = inp("ln_b", [1, D])
    wr_in = inp("w_router", [D, E]); wg_in = inp("w_gating", [D, E])
    wgate_in = inp("w_gate", [E, D, FF]); wup_in = inp("w_up", [E, D, FF])
    wdown_in = inp("w_down", [E, FF, D])
    wd1_in = inp("W_down1", [D, D])
    mL_in = inp("maskL", [128, 128]); mU_in = inp("maskU", [128, 128])
    idf_in = inp("idf", [128, 128]); idb_in = inp("idb", [128, 128], BF16)
    consts_in = inp("consts", [128, len(CONST_VALS)])

    x2_out = outp("x2_half", [TH, D])
    xd_out = outp("xd_half", [TH, D], BF16)
    if dbg:
        o_dbg = outp("o_dbg", [T, D], BF16)
        x1_dbg = outp("x1_dbg", [TH, D])

    o_scr = nc.dram_tensor("o_scr", [T, D], BF16).ap()

    p = Prog()
    with ExitStack() as CTX:
        def sb(name, shape, dt): return CTX.enter_context(nc.sbuf_tensor(name, shape, dt)).ap()
        mL = sb("mL", [128, 128], F32); mU = sb("mU", [128, 128], F32)
        idf = sb("idf_s", [128, 128], F32); idb = sb("idb_s", [128, 128], BF16)
        ones_f = sb("ones_f", [1, 128], F32)
        ones_c = sb("ones_c", [128, 1], F32)
        p.g(lambda g: g.dma_start(out=mL, in_=mL_in), writes=["mL"], dma=True, tag="ldmL")
        p.g(lambda g: g.dma_start(out=mU, in_=mU_in), writes=["mU"], dma=True, tag="ldmU")
        p.g(lambda g: g.dma_start(out=idf, in_=idf_in), writes=["idf"], dma=True, tag="ldidf")
        p.g(lambda g: g.dma_start(out=idb, in_=idb_in), writes=["idb"], dma=True, tag="ldidb")
        p.g(lambda g: g.memset(ones_f, 1.0), writes=["ones_f"], tag="onesf")
        p.g(lambda g: g.memset(ones_c, 1.0), writes=["ones_c"], tag="onesc")
        const_sb = sb("const_sb", [128, len(CONST_VALS)], F32)
        p.g(lambda g: g.dma_start(out=const_sb, in_=consts_in), writes=["consts"], dma=True, tag="ldconsts")
        for _i, _v in enumerate(CONST_VALS):
            nc.const_aps.register(F32, _v, const_sb[:, _i:_i + 1])

        with ExitStack() as C1:
            def sb1(name, shape, dt): return C1.enter_context(nc.sbuf_tensor(name, shape, dt)).ap()
            def ps1_(name, shape, dt=F32): return C1.enter_context(nc.psum_tensor(name, shape, dt)).ap()
            Wq_bf = sb1("Wq_bf", [128, 8 * D], BF16)
            Wk_bf = sb1("Wk_bf", [128, 8 * D], BF16)
            Wv_bf = sb1("Wv_bf", [128, 8 * D], BF16)
            Wb_bf = sb1("Wb_bf", [128, 32], BF16)
            conv_sb = {"q": sb1("cq", [128, 32], F32), "k": sb1("ck", [128, 32], F32),
                       "v": sb1("cv", [128, 32], F32)}
            grms_row = sb1("grms_row", [1, DH], F32)
            grms_bc = sb1("grms_bc", [128, DH], F32)
            x_gw = sb1("x_gw", [128, 4 * D], F32)
            xln_c = sb1("xln_c", [128, D], BF16)
            xln_fm = sb1("xln_fm", [128, 8 * 512], BF16)
            q_fm = sb1("q_fm", [128, 8 * 512], BF16)
            k_fm = sb1("k_fm", [128, 8 * 512], BF16)
            v_fm = sb1("v_fm", [128, 8 * 512], BF16)
            beta_g = sb1("beta_g", [128, 16], F32)
            k_tmc = sb1("k_tmc", [128, D], BF16)
            v_tmc = sb1("v_tmc", [128, D], BF16)
            ext = sb1("ext", [128, 516], F32)
            tails = sb1("tails", [128, 72], F32)
            cacc = [sb1("cacc0", [128, 512], F32), sb1("cacc1", [128, 512], F32)]
            sl = [sb1("sl0", [128, 512], F32), sb1("sl1", [128, 512], F32)]
            sqb = sb1("sqb", [128, 512], F32)
            rs_row = sb1("rs_row", [1, 512], F32)
            sv_row = sb1("sv_row", [1, 512], F32)
            S_all = sb1("S_all", [128, H4 * 512], F32)
            Sbf_all = sb1("Sbf_all", [128, H4 * 512], BF16)
            Mb = sb1("Mb", [128, 128], F32); MTb = sb1("MTb", [128, 128], F32)
            Pb = [sb1("P0b", [128, 128], F32), sb1("P1b", [128, 128], F32)]
            PTb = [sb1("PT0b", [128, 128], F32), sb1("PT1b", [128, 128], F32)]
            t256 = sb1("t256", [128, 256], F32)
            cur = [sb1("cur0b", [128, 256], F32), sb1("cur1b", [128, 256], F32)]
            D_bf = sb1("D_bfb", [128, 256], BF16)
            Wmb = sb1("Wmb", [128, 128], BF16)
            ssq_c = sb1("ssq_c", [128, 1], F32)
            sv_c = sb1("sv_c", [128, 1], F32)
            rst_c = sb1("rst_c", [128, 1], F32)
            osq = sb1("osq", [128, 256], F32)
            o_stage = [sb1("o_st0", [128, D], BF16), sb1("o_st1", [128, D], BF16)]
            lnsc = mk_ln_scratch(nc, C1, "p1ln")

            ps_p = ps1_("ps_p", [128, 512])
            ps_bc = ps1_("ps_bc", [128, 512])
            ps_tr = ps1_("ps_tr", [128, 128], BF16)
            ps_trf = ps1_("ps_trf", [128, 128])
            ps_a = ps1_("ps_a", [128, 128])
            ps_b = ps1_("ps_b", [128, 128])
            ps_c = ps1_("ps_c", [128, 256])
            ps_d = ps1_("ps_d", [128, 512])

            for dc in range(8):
                for nm, Wbuf, Win in (("q", Wq_bf, Wq_in), ("k", Wk_bf, Wk_in), ("v", Wv_bf, Wv_in)):
                    p.g(lambda g, dc=dc, Wbuf=Wbuf, Win=Win: g.dma_start(
                        out=Wbuf[:, dc * D:(dc + 1) * D], in_=Win[dc * 128:(dc + 1) * 128, :]),
                        writes=[(f"W{nm}", dc)], dma=True, tag=f"ldW{nm}{dc}")
                p.g(lambda g, dc=dc: g.dma_start(
                    out=Wb_bf[:, dc * 4:(dc + 1) * 4], in_=Wb_in[dc * 128:(dc + 1) * 128, :]),
                    writes=[("Wb", dc)], dma=True, tag=f"ldWb{dc}")
                for nm, Cin in (("q", cq_in), ("k", ck_in), ("v", cv_in)):
                    p.g(lambda g, dc=dc, nm=nm, Cin=Cin: g.dma_start(
                        out=conv_sb[nm][:, dc * 4:(dc + 1) * 4], in_=Cin[dc * 128:(dc + 1) * 128, :]),
                        writes=[(f"conv{nm}", dc)], dma=True, tag=f"ldc{nm}{dc}")
            p.g(lambda g: g.dma_start(out=grms_row, in_=grms_in), writes=["grms_row"], dma=True, tag="ldgr")
            p.t(lambda t: t.matmul(ps_bc[:, 0:DH], ones_f, grms_row, start=True, stop=True),
                reads=["ones_f", "grms_row"], writes=["ps_bc"], tag="grbc")
            p.v(lambda v: v.tensor_copy(grms_bc, ps_bc[:, 0:DH]),
                reads=["ps_bc"], writes=["grms_bc"], tag="grbcev")
            p.g(lambda g: g.memset(tails, 0.0), writes=["tails"], tag="tails0")
            p.g(lambda g: g.memset(S_all, 0.0), writes=[("S", h) for h in range(H4)], tag="S0")
            p.g(lambda g: g.memset(Sbf_all, 0.0), writes=[("Sbf", h) for h in range(H4)], tag="Sbf0")

            PROJ = {"q": (Wq_bf, q_fm, True), "k": (Wk_bf, k_fm, True), "v": (Wv_bf, v_fm, False)}

            for g_i in range(NG):
                p.sy(lambda s, g_i=g_i: s.dma_start(
                    out=x_gw.rearrange("p (tl d) -> p tl d", d=D),
                    in_=x_in[g_i * 512:(g_i + 1) * 512, :]
                    .rearrange("(tl p) d -> p tl d", p=128)),
                    writes=["x_gw"], dma=True, tag=f"ldxg{g_i}")
                for tl in range(4):
                    emit_ln_chunk(p, x_gw[:, tl * D:(tl + 1) * D], "x_gw", lnsc,
                                  xln_c[:, :], ["xln_c"], f"g{g_i}t{tl}")
                    for dc in range(8):
                        p.t(lambda t, dc=dc: t.transpose(
                            ps_tr[:, :], xln_c[:, dc * 128:(dc + 1) * 128], idb),
                            reads=["xln_c", "idb"], writes=["ps_tr"], tag=f"xt{g_i}{tl}{dc}")
                        p.v(lambda v, dc=dc, tl=tl: v.tensor_copy(
                            xln_fm[:, dc * 512 + tl * 128: dc * 512 + (tl + 1) * 128], ps_tr[:, :]),
                            reads=["ps_tr"], writes=[("xfm", dc, tl)], tag=f"xte{g_i}{tl}{dc}")
                    for dc in range(8):
                        p.t(lambda t, dc=dc, tl=tl: t.matmul(
                            ps_bc[0:128, 0:4],
                            xln_fm[:, dc * 512 + tl * 128: dc * 512 + (tl + 1) * 128],
                            Wb_bf[:, dc * 4:(dc + 1) * 4],
                            start=(dc == 0), stop=(dc == 7)),
                            reads=[("xfm", dc, tl), ("Wb", dc)], writes=["ps_bc"], tag=f"bmm{g_i}{tl}{dc}")
                    p.s(lambda s, tl=tl: s.activation(
                        beta_g[:, tl * 4:(tl + 1) * 4], ps_bc[0:128, 0:4], AF.Sigmoid),
                        reads=["ps_bc"], writes=[("beta", tl)], tag=f"bsig{g_i}{tl}")
                for nm in ("q", "k", "v"):
                    Wbuf, outbuf, do_norm = PROJ[nm]
                    for dco in range(8):
                        pp = dco % 2
                        for dci in range(8):
                            p.t(lambda t, dci=dci, dco=dco, Wbuf=Wbuf: t.matmul(
                                ps_p[:, :],
                                Wbuf[:, dci * D + dco * 128: dci * D + (dco + 1) * 128],
                                xln_fm[:, dci * 512:(dci + 1) * 512],
                                start=(dci == 0), stop=(dci == 7)),
                                reads=[(f"W{nm}", dci)] + [("xfm", dci, tl) for tl in range(4)],
                                writes=["ps_p"], tag=f"p{nm}{g_i}{dco}{dci}")
                        tslot = ("q", "k", "v").index(nm) * 24 + dco * 3
                        p.v(lambda v, tslot=tslot: v.tensor_copy(
                            ext[:, 0:3], tails[:, tslot:tslot + 3]),
                            reads=["tails"], writes=["ext_h"], tag=f"ti{nm}{g_i}{dco}")
                        p.v(lambda v: v.tensor_copy(ext[:, 3:515], ps_p[:, :]),
                            reads=["ps_p"], writes=["ext"], tag=f"pe{nm}{g_i}{dco}")
                        p.v(lambda v, tslot=tslot: v.tensor_copy(
                            tails[:, tslot:tslot + 3], ext[:, 512:515]),
                            reads=["ext"], writes=["tails"], tag=f"to{nm}{g_i}{dco}")
                        cv = conv_sb[nm]
                        p.v(lambda v, dco=dco, cv=cv: v.tensor_scalar(
                            cacc[0][:, :], ext[:, 0:512], cv[:, dco * 4:dco * 4 + 1], None, op0=ALU.mult),
                            reads=["ext", "ext_h", (f"conv{nm}", dco)],
                            writes=["cacc0"], tag=f"c0{nm}{g_i}{dco}")
                        for j in (1, 2, 3):
                            src, dst = cacc[(j + 1) % 2], cacc[j % 2]
                            p.v(lambda v, j=j, dco=dco, cv=cv, src=src, dst=dst: v.scalar_tensor_tensor(
                                dst[:, :], ext[:, j:j + 512], cv[:, dco * 4 + j:dco * 4 + j + 1],
                                src[:, :], op0=ALU.mult, op1=ALU.add),
                                reads=["ext", "ext_h", f"cacc{(j + 1) % 2}", (f"conv{nm}", dco)],
                                writes=[f"cacc{j % 2}"], tag=f"c{j}{nm}{g_i}{dco}")
                        if nm == "v":
                            p.s(lambda s, dco=dco: s.activation(
                                v_fm[:, dco * 512:(dco + 1) * 512], cacc[1][:, :], AF.Silu),
                                reads=["cacc1"], writes=[("vfm", dco)], tag=f"sl{nm}{g_i}{dco}")
                        else:
                            p.s(lambda s, pp=pp: s.activation(sl[pp][:, :], cacc[1][:, :], AF.Silu),
                                reads=["cacc1"], writes=[f"sl{pp}"], tag=f"sl{nm}{g_i}{dco}")
                            p.s(lambda s, pp=pp: s.activation(sqb[:, :], sl[pp][:, :], AF.Square),
                                reads=[f"sl{pp}"], writes=["sqb"], tag=f"sq{nm}{g_i}{dco}")
                            p.t(lambda t, pp=pp: t.matmul(
                                ps_bc[0:1, 0:512], ones_c, sqb[:, :],
                                start=(pp == 0), stop=(pp == 1)),
                                reads=["sqb", "ones_c"], writes=["ps_bc"], tag=f"nrm{nm}{g_i}{dco}")
                            if pp == 1:
                                p.s(lambda s: s.activation(sv_row[:, :], ps_bc[0:1, 0:512], AF.Sqrt, bias=1e-6),
                                    reads=["ps_bc", "consts"], writes=["sv_row"], tag=f"nsq{nm}{g_i}{dco}")
                                p.v(lambda v: v.reciprocal(rs_row[:, :], sv_row[:, :]),
                                    reads=["sv_row"], writes=["rs_row"], tag=f"nrc{nm}{g_i}{dco}")
                                p.t(lambda t: t.matmul(ps_bc[:, :], ones_f, rs_row[:, :], start=True, stop=True),
                                    reads=["rs_row", "ones_f"], writes=["ps_bc"], tag=f"nbc{nm}{g_i}{dco}")
                                for q_dc, q_pp in ((dco - 1, 0), (dco, 1)):
                                    p.v(lambda v, q_dc=q_dc, q_pp=q_pp, outbuf=outbuf: v.tensor_tensor(
                                        outbuf[:, q_dc * 512:(q_dc + 1) * 512], sl[q_pp][:, :],
                                        ps_bc[:, :], op=ALU.mult),
                                        reads=[f"sl{q_pp}", "ps_bc"],
                                        writes=[(f"{nm}fm", q_dc)], tag=f"nn{nm}{g_i}{q_dc}")
                for tl in range(4):
                    c_glob = g_i * 4 + tl
                    for dc in range(8):
                        p.t(lambda t, dc=dc, tl=tl: t.transpose(
                            ps_tr[:, :], k_fm[:, dc * 512 + tl * 128: dc * 512 + (tl + 1) * 128], idb),
                            reads=[("kfm", dc), "idb"], writes=["ps_tr"], tag=f"ktt{c_glob}{dc}")
                        p.v(lambda v, dc=dc: v.tensor_copy(
                            k_tmc[:, dc * 128:(dc + 1) * 128], ps_tr[:, :]),
                            reads=["ps_tr"], writes=["k_tmc"], tag=f"kte{c_glob}{dc}")
                        p.t(lambda t, dc=dc, tl=tl: t.transpose(
                            ps_tr[:, :], v_fm[:, dc * 512 + tl * 128: dc * 512 + (tl + 1) * 128], idb),
                            reads=[("vfm", dc), "idb"], writes=["ps_tr"], tag=f"vtt{c_glob}{dc}")
                        p.v(lambda v, dc=dc: v.tensor_copy(
                            v_tmc[:, dc * 128:(dc + 1) * 128], ps_tr[:, :]),
                            reads=["ps_tr"], writes=["v_tmc"], tag=f"vte{c_glob}{dc}")
                    for h in range(H4):
                        emit_scan_head(p, tl, h, c_glob,
                                       (q_fm, k_fm, k_tmc, v_tmc, beta_g, S_all, Sbf_all,
                                        Mb, MTb, Pb, PTb, t256, cur, D_bf, Wmb, mL, mU, idf,
                                        ps_a, ps_b, ps_c, ps_d, ps_trf, osq, ssq_c, sv_c,
                                        rst_c, grms_bc, o_stage[tl % 2], ("ostg", tl % 2)))
                    p.sy(lambda s, c_glob=c_glob, tl=tl: s.dma_start(
                        out=o_scr[c_glob * 128:(c_glob + 1) * 128, :], in_=o_stage[tl % 2][:, :]),
                        reads=[("ostg", tl % 2)], writes=[("oscr", c_glob)], dma=True, tag=f"sto{c_glob}")
        p.set_fence()

        x_tm = sb("x_tm", [128, NTC * D], F32)
        p.sy(lambda s: s.dma_start(
            out=x_tm.rearrange("p (tc d) -> p tc d", d=D),
            in_=xh_in.rearrange("(tc p) d -> p tc d", p=128)),
            writes=[("x", tc) for tc in range(NTC)], dma=True, tag="ldxh")

        with ExitStack() as C2:
            def sb2(name, shape, dt): return C2.enter_context(nc.sbuf_tensor(name, shape, dt)).ap()
            def ps2_(name, shape, dt=F32): return C2.enter_context(nc.psum_tensor(name, shape, dt)).ap()
            o_half = sb2("o_half", [128, NTC * D], BF16)
            o_fm = sb2("o_fm", [128, 8 * TH], BF16)
            Wo_bf = sb2("Wo_bf", [128, 8 * D], BF16)
            lng_bc = sb2("lng_bc", [128, D], F32)
            lnb_bc = sb2("lnb_bc", [128, D], F32)
            lrow = sb2("lrow", [1, D], F32)
            t1b = sb2("t1b", [128, D], F32)
            t2b = sb2("t2b", [128, D], F32)
            lnsc2 = mk_ln_scratch(nc, C2, "p2ln")
            ps_y = ps2_("ps_y", [128, D])
            ps_tr2 = ps2_("ps_tr2", [128, 128], BF16)
            ps_r2 = ps2_("ps_r2", [128, D])

            for dc in range(8):
                p.g(lambda g, dc=dc: g.dma_start(
                    out=Wo_bf[:, dc * D:(dc + 1) * D], in_=Wo_in[dc * 128:(dc + 1) * 128, :]),
                    writes=[("Wo", dc)], dma=True, tag=f"ldWo{dc}")
            p.g(lambda g: g.dma_start(out=lrow, in_=lng_in), writes=["lrow"], dma=True, tag="ldlng")
            for half in range(2):
                p.t(lambda t, half=half: t.matmul(
                    ps_r2[:, half * 512:(half + 1) * 512], ones_f,
                    lrow[:, half * 512:(half + 1) * 512], start=True, stop=True),
                    reads=["lrow", "ones_f"], writes=["ps_r2"], tag=f"lngbc{half}")
            p.v(lambda v: v.tensor_copy(lng_bc, ps_r2),
                reads=["ps_r2"], writes=["lng_bc"], tag="lngev")
            p.g(lambda g: g.dma_start(out=lrow, in_=lnb_in),
                reads=["ps_r2"], writes=["lrow"], dma=True, tag="ldlnb")
            for half in range(2):
                p.t(lambda t, half=half: t.matmul(
                    ps_r2[:, half * 512:(half + 1) * 512], ones_f,
                    lrow[:, half * 512:(half + 1) * 512], start=True, stop=True),
                    reads=["lrow", "ones_f"], writes=["ps_r2"], tag=f"lnbbc{half}")
            p.v(lambda v: v.tensor_copy(lnb_bc, ps_r2),
                reads=["ps_r2"], writes=["lnb_bc"], tag="lnbev")

            s_sv = sb2("s_sv", [1, 1], U32)
            p.g(lambda g: g.dma_start(out=s_sv, in_=s_in), writes=["s_sv"], dma=True, tag="ldssv")
            cell = {}
            def ld_s(g):
                reg = g.alloc_register("sreg")
                g.load(reg, s_sv[0:1, 0:1])
                cell["s"] = g.snap(reg, min_val=0, max_val=1)
                return g.nop()
            p.g(ld_s, reads=["s_sv"], writes=["sreg"], tag="ldsreg")
            p.g(lambda g: g.dma_start(
                out=o_half.rearrange("p (tc d) -> p tc d", d=D),
                in_=o_scr.rearrange("(c p) d -> p c d", p=128)[:, ds(cell["s"] * NTC, NTC), :]),
                reads=["sreg"] + [("oscr", c) for c in range(NCH)],
                writes=[("oh", tc) for tc in range(NTC)], dma=True, tag="ldoh")
            for tc in range(NTC):
                for dc in range(8):
                    p.t(lambda t, tc=tc, dc=dc: t.transpose(
                        ps_tr2[:, :], o_half[:, tc * D + dc * 128: tc * D + (dc + 1) * 128], idb),
                        reads=[("oh", tc), "idb"], writes=["ps_tr2"], tag=f"ot{tc}{dc}")
                    p.v(lambda v, tc=tc, dc=dc: v.tensor_copy(
                        o_fm[:, dc * TH + tc * 128: dc * TH + (tc + 1) * 128], ps_tr2[:, :]),
                        reads=["ps_tr2"], writes=[("ofm", dc, tc)], tag=f"ote{tc}{dc}")
            for tc in range(NTC):
                for half in range(2):
                    for dc in range(8):
                        p.t(lambda t, tc=tc, dc=dc, half=half: t.matmul(
                            ps_y[:, half * 512:(half + 1) * 512],
                            o_fm[:, dc * TH + tc * 128: dc * TH + (tc + 1) * 128],
                            Wo_bf[:, dc * D + half * 512: dc * D + half * 512 + 512],
                            start=(dc == 0), stop=(dc == 7)),
                            reads=[("ofm", dc, tc), ("Wo", dc)], writes=["ps_y"], tag=f"yo{tc}{half}{dc}")
                xap = x_tm[:, tc * D:(tc + 1) * D]
                pref = f"p2a{tc}"
                xck, rstk = emit_ln_partial(p, xap, ("x", tc), lnsc2, pref)
                p.v(lambda v: v.scalar_tensor_tensor(
                    t1b[:, :], lnsc2["xc"], lnsc2["rstd"], ps_y[:, :], op0=ALU.mult, op1=ALU.add),
                    reads=[xck, rstk, "ps_y"], writes=["t1b"], tag=pref + "t1")
                pref = f"p2b{tc}"
                xck, rstk = emit_ln_partial(p, t1b[:, :], "t1b", lnsc2, pref)
                p.v(lambda v: v.tensor_scalar(t2b[:, :], lnsc2["xc"], lnsc2["rstd"], None, op0=ALU.mult),
                    reads=[xck, rstk], writes=["t2b"], tag=pref + "ap")
                p.v(lambda v: v.tensor_tensor(t2b[:, :], t2b[:, :], lng_bc, op=ALU.mult),
                    reads=["t2b", "lng_bc"], writes=["t2b"], tag=pref + "g")
                p.v(lambda v: v.tensor_tensor(t2b[:, :], t2b[:, :], lnb_bc, op=ALU.add),
                    reads=["t2b", "lnb_bc"], writes=["t2b"], tag=pref + "b")
                p.v(lambda v, xap=xap: v.tensor_tensor(xap, xap, t2b[:, :], op=ALU.add),
                    reads=[("x", tc), "t2b"], writes=[("x", tc)], tag=pref + "res")
            if dbg:
                p.dbg_toks = []
                for c in range(NCH):
                    p.dbg_toks.append(p.sy(lambda s, c=c: s.dma_start(
                        out=o_dbg[c * 128:(c + 1) * 128, :], in_=o_scr[c * 128:(c + 1) * 128, :]),
                        reads=[("oscr", c)], writes=[("odbg", c)], dma=True, tag=f"odbg{c}"))
                for tc in range(NTC):
                    p.dbg_toks.append(p.sy(lambda s, tc=tc: s.dma_start(
                        out=x1_dbg[tc * 128:(tc + 1) * 128, :], in_=x_tm[:, tc * D:(tc + 1) * D]),
                        reads=[("x", tc)], writes=[("x1dbg", tc)], dma=True, tag=f"x1dbg{tc}"))
        p.set_fence()

        emit_moe(p, nc, CTX, "l0f", x_tm, idb, idf, ones_f,
                 dict(kind="glu", wr_in=wr_in, wg_in=wg_in,
                      w1_in=wgate_in, wu_in=wup_in, w2_in=wdown_in,
                      br_in=None, b1_in=None, b2_in=None))
        p.set_fence()

        with ExitStack() as C4:
            def sb4(name, shape, dt): return C4.enter_context(nc.sbuf_tensor(name, shape, dt)).ap()
            def ps4_(name, shape, dt=F32): return C4.enter_context(nc.psum_tensor(name, shape, dt)).ap()
            xd_fm = sb4("p4_fm", [128, 8 * TH], BF16)
            Wd1_bf = sb4("Wd1_bf", [128, 8 * D], BF16)
            xln_c4 = sb4("xln_c4", [128, D], BF16)
            xd_stage = sb4("xd_stage", [128, D], BF16)
            lnsc4 = mk_ln_scratch(nc, C4, "p4ln")
            ps_y4 = ps4_("ps_y4", [128, D])
            ps_tr4 = ps4_("ps_tr4", [128, 128], BF16)
            for dc in range(8):
                p.g(lambda g, dc=dc: g.dma_start(
                    out=Wd1_bf[:, dc * D:(dc + 1) * D], in_=wd1_in[dc * 128:(dc + 1) * 128, :]),
                    writes=[("Wd1", dc)], dma=True, tag=f"ldWd1{dc}")
            st_toks = []
            for tc in range(NTC):
                emit_ln_chunk(p, x_tm[:, tc * D:(tc + 1) * D], ("x", tc), lnsc4,
                              xln_c4[:, :], ["xln_c4"], f"p4l{tc}")
                for dc in range(8):
                    p.t(lambda t, dc=dc: t.transpose(
                        ps_tr4[:, :], xln_c4[:, dc * 128:(dc + 1) * 128], idb),
                        reads=["xln_c4", "idb"], writes=["ps_tr4"], tag=f"p4t{tc}{dc}")
                    p.v(lambda v, tc=tc, dc=dc: v.tensor_copy(
                        xd_fm[:, dc * TH + tc * 128: dc * TH + (tc + 1) * 128], ps_tr4[:, :]),
                        reads=["ps_tr4"], writes=[("p4fm", dc, tc)], tag=f"p4te{tc}{dc}")
                st_toks.append(p.sy(lambda s, tc=tc: s.dma_start(
                    out=x2_out[tc * 128:(tc + 1) * 128, :], in_=x_tm[:, tc * D:(tc + 1) * D]),
                    reads=[("x", tc)], dma=True, tag=f"stx2{tc}"))
            for tc in range(NTC):
                for half in range(2):
                    for dc in range(8):
                        p.t(lambda t, tc=tc, dc=dc, half=half: t.matmul(
                            ps_y4[:, half * 512:(half + 1) * 512],
                            xd_fm[:, dc * TH + tc * 128: dc * TH + (tc + 1) * 128],
                            Wd1_bf[:, dc * D + half * 512: dc * D + half * 512 + 512],
                            start=(dc == 0), stop=(dc == 7)),
                            reads=[("p4fm", dc, tc), ("Wd1", dc)], writes=["ps_y4"], tag=f"xd{tc}{half}{dc}")
                p.v(lambda v: v.tensor_copy(xd_stage[:, :], ps_y4[:, :]),
                    reads=["ps_y4"], writes=["xd_stage"], tag=f"xde{tc}")
                st_toks.append(p.sy(lambda s, tc=tc: s.dma_start(
                    out=xd_out[tc * 128:(tc + 1) * 128, :], in_=xd_stage[:, :]),
                    reads=["xd_stage"], dma=True, tag=f"stxd{tc}"))
            p.sy(lambda s: s.nop(), extra_deps=st_toks + getattr(p, "dbg_toks", []), tag="fin")

        p.finalize(nc, CTX)
        with nc.Block() as block:
            p.emit(block)
    return nc


def emit_ln_partial(p, x_ap, x_key, scr, pref, n=1024, eps=1e-5):
    inv_n = 1.0 / n
    nm = scr["_n"]
    p.v(lambda v: v.reduce_sum(scr["s1"], x_ap, axis=AX.X),
        reads=[x_key], writes=[nm + "s1"], tag=pref + "sum")
    p.v(lambda v: v.tensor_scalar(scr["nmu"], scr["s1"], -inv_n, None, op0=ALU.mult),
        reads=[nm + "s1"], writes=[nm + "nmu"], tag=pref + "nmu")
    p.v(lambda v: v.tensor_scalar(scr["xc"], x_ap, scr["nmu"], None, op0=ALU.add),
        reads=[x_key, nm + "nmu"], writes=[nm + "xc"], tag=pref + "xc")
    p.s(lambda s: s.activation(scr["xsq"], scr["xc"], AF.Square, accum_out=scr["ssq"]),
        reads=[nm + "xc"], writes=[nm + "xsq", nm + "ssq"], tag=pref + "sq")
    p.s(lambda s: s.activation(scr["sv"], scr["ssq"], AF.Sqrt, bias=float(eps), scale=inv_n),
        reads=[nm + "ssq", "consts"], writes=[nm + "sv"], tag=pref + "sv")
    p.v(lambda v: v.reciprocal(scr["rstd"], scr["sv"]),
        reads=[nm + "sv"], writes=[nm + "rstd"], tag=pref + "rstd")
    return nm + "xc", nm + "rstd"



def build_launch2(dbg=False, phases=("q1", "q2", "q3")):
    nc = bass.Bass("TRN2", target_bir_lowering=False)
    def inp(name, shape, dt=F32):
        return nc.dram_tensor(name, shape, dt, kind="ExternalInput").ap()
    def outp(name, shape, dt=F32):
        return nc.dram_tensor(name, shape, dt, kind="ExternalOutput").ap()

    xd_in = inp("xd_full", [T, D], BF16)
    x2_in = inp("x2_half", [TH, D])
    tk_in = inp("tokeysT", [16, 64, 64])
    tq_in = inp("toqueriesT", [16, 64, 64])
    tv_in = inp("tovaluesT", [16, 64, 64])
    wuni_in = inp("W_unify", [D, D])
    bu_in = inp("b_unify", [1, D])
    wr1_in = inp("Wr1", [D, E])
    br1_in = inp("br1", [1, E])
    w11_in = inp("W11", [E, D, FF])
    b11_in = inp("b11", [E, 128, 32])
    w21_in = inp("W21", [E, FF, D])
    b21_in = inp("b21", [E, D])
    mL_in = inp("maskL", [128, 128]); mU_in = inp("maskU", [128, 128])
    idf_in = inp("idf", [128, 128]); idb_in = inp("idb", [128, 128], BF16)
    consts_in = inp("consts", [128, len(CONST_VALS)])
    x4_out = outp("x4_half", [TH, D])
    if dbg:
        on_dbg = outp("on_dbg", [TH, D], BF16)
        x3_dbg = outp("x3_dbg", [TH, D])

    SC2 = float(64.0 ** (-0.5))

    p = Prog()
    with ExitStack() as CTX:
        def sb(name, shape, dt): return CTX.enter_context(nc.sbuf_tensor(name, shape, dt)).ap()
        idb = sb("idb_s", [128, 128], BF16)
        idf = sb("idf_s", [128, 128], F32)
        ones_f = sb("ones_f", [1, 128], F32)
        p.g(lambda g: g.dma_start(out=idb, in_=idb_in), writes=["idb"], dma=True, tag="ldidb")
        p.g(lambda g: g.dma_start(out=idf, in_=idf_in), writes=["idf"], dma=True, tag="ldidf")
        p.g(lambda g: g.memset(ones_f, 1.0), writes=["ones_f"], tag="ones")
        const_sb = sb("const_sb", [128, len(CONST_VALS)], F32)
        p.g(lambda g: g.dma_start(out=const_sb, in_=consts_in), writes=["consts"], dma=True, tag="ldconsts")
        for _i, _v in enumerate(CONST_VALS):
            nc.const_aps.register(F32, _v, const_sb[:, _i:_i + 1])
        x_tm = sb("x_tm", [128, NTC * D], F32)
        p.sy(lambda s: s.dma_start(
            out=x_tm.rearrange("p (tc d) -> p tc d", d=D),
            in_=x2_in.rearrange("(tc p) d -> p tc d", p=128)),
            writes=[("x", tc) for tc in range(NTC)], dma=True, tag="ldx2")

        C12 = CTX.enter_context(ExitStack())
        out_n = C12.enter_context(nc.sbuf_tensor("out_n", [128, NTC * D], BF16)).ap()
        p.g(lambda g: g.memset(out_n, 0.0),
            writes=[("on", tcg, hh) for tcg in range(NTC) for hh in range(16)], tag="on0")
        if "q1" in phases:
          with ExitStack() as C1:
              def sb1(name, shape, dt): return C1.enter_context(nc.sbuf_tensor(name, shape, dt)).ap()
              def ps1_(name, shape, dt=F32): return C1.enter_context(nc.psum_tensor(name, shape, dt)).ap()
              xd_fm = sb1("xd_fm", [128, 8 * T], BF16)
              ps_pj = ps1_("ps_pj", [64, 512])
              ps_vt = ps1_("ps_vt", [128, 64])
              ps_bd = ps1_("ps_bd", [1, 512])
              ps_st = [ps1_("ps_st0", [128, 512]), ps1_("ps_st1", [128, 512])]
              ps_pv = ps1_("ps_pv", [65, 512])
              ps_tr1 = ps1_("ps_tr1", [128, 128], BF16)
              ps_rc = ps1_("ps_rc", [128, 1])
              with nc.sbuf_tensor("xd_tm", [128, NTC * D], BF16) as xd_tm_:
                  xd_tm = xd_tm_.ap()
                  for hf in range(2):
                      p.g(lambda g, hf=hf: g.dma_start(
                          out=xd_tm.rearrange("p (tc d) -> p tc d", d=D),
                          in_=xd_in[hf * TH:(hf + 1) * TH, :].rearrange("(tc p) d -> p tc d", p=128)),
                          writes=["xd_tm"], dma=True, tag=f"ldxd{hf}")
                      for tl in range(NTC):
                          tc = hf * NTC + tl
                          for dc in range(8):
                              p.t(lambda t, tl=tl, dc=dc, xd_tm=xd_tm: t.transpose(
                                  ps_tr1[:, :], xd_tm[:, tl * D + dc * 128: tl * D + (dc + 1) * 128], idb),
                                  reads=["xd_tm", "idb"], writes=["ps_tr1"], tag=f"q0t{tc}{dc}")
                              p.v(lambda v, tc=tc, dc=dc: v.tensor_copy(
                                  xd_fm[:, dc * T + tc * 128: dc * T + (tc + 1) * 128], ps_tr1[:, :]),
                                  reads=["ps_tr1"], writes=[("xdfm", dc, tc)], tag=f"q0te{tc}{dc}")
              p.set_fence()
              KT = sb1("KT", [65, T], BF16)
              QT = sb1("QT", [65, TH], BF16)
              absQ = sb1("absQ", [64, 512], BF16)
              Vau = sb1("Vau", [128, NCH * 65], BF16)
              Mi = sb1("Mi", [64, 1], BF16)
              tkb = sb1("tkb", [128, 64], BF16)
              tqb = sb1("tqb", [128, 64], BF16)
              tvb = sb1("tvb", [128, 64], BF16)
              P_sb = sb1("P_sb", [128, 16 * 512], BF16)
              rcol = sb1("rcol", [128, 1], F32)
              rs_sb = sb1("rs_sb", [1, 512], F32)
              of_h = sb1("of_h", [64, 512], BF16)
              p.g(lambda g: g.memset(Vau, 1.0), writes=["Vau_ones"], tag="vau1")
              p.g(lambda g: g.memset(KT[64:65, :], 1.0), writes=["KTones"], tag="kt1")

              for h in range(16):
                  dch = h // 2
                  pr = (h % 2) * 64
                  p.g(lambda g, h=h, pr=pr: g.dma_start(out=tkb[pr:pr + 64, :], in_=tk_in[h, :, :]),
                      writes=["tkb"], dma=True, tag=f"ltk{h}")
                  p.g(lambda g, h=h, pr=pr: g.dma_start(out=tqb[pr:pr + 64, :], in_=tq_in[h, :, :]),
                      writes=["tqb"], dma=True, tag=f"ltq{h}")
                  p.g(lambda g, h=h, pr=pr: g.dma_start(out=tvb[pr:pr + 64, :], in_=tv_in[h, :, :]),
                      writes=["tvb"], dma=True, tag=f"ltv{h}")
                  for tg in range(8):
                      p.t(lambda t, tg=tg, dch=dch, pr=pr: t.matmul(
                          ps_pj[:, :], tkb[pr:pr + 64, :],
                          xd_fm[pr:pr + 64, dch * T + tg * 512: dch * T + (tg + 1) * 512],
                          start=True, stop=True),
                          reads=["tkb"] + [("xdfm", dch, tg * 4 + i) for i in range(4)],
                          writes=["ps_pj"], tag=f"kp{h}{tg}")
                      p.s(lambda s, tg=tg: s.activation(
                          KT[0:64, tg * 512:(tg + 1) * 512], ps_pj[:, :], AF.Copy, scale=SC2),
                          reads=["ps_pj", "consts"], writes=[("KTh", tg)], tag=f"kpe{h}{tg}")
                  for tg in range(4):
                      p.t(lambda t, tg=tg, dch=dch, pr=pr: t.matmul(
                          ps_pj[:, :], tqb[pr:pr + 64, :],
                          xd_fm[pr:pr + 64, dch * T + tg * 512: dch * T + (tg + 1) * 512],
                          start=True, stop=True),
                          reads=["tqb"] + [("xdfm", dch, tg * 4 + i) for i in range(4)],
                          writes=["ps_pj"], tag=f"qp{h}{tg}")
                      p.s(lambda s, tg=tg: s.activation(
                          QT[0:64, tg * 512:(tg + 1) * 512], ps_pj[:, :], AF.Copy),
                          reads=["ps_pj"], writes=[("QTh", tg)], tag=f"qpe{h}{tg}")

                  for tc in range(NCH):
                      p.t(lambda t, tc=tc, dch=dch, pr=pr: t.matmul(
                          ps_vt[:, :],
                          xd_fm[pr:pr + 64, dch * T + tc * 128: dch * T + (tc + 1) * 128],
                          tvb[pr:pr + 64, :], start=True, stop=True),
                          reads=["tvb", ("xdfm", dch, tc)], writes=["ps_vt"], tag=f"vp{h}{tc}")
                      p.v(lambda v, tc=tc: v.tensor_copy(
                          Vau[:, tc * 65: tc * 65 + 64], ps_vt[:, :]),
                          reads=["ps_vt"], writes=[("Vau", tc)], tag=f"vpe{h}{tc}")
                  p.v(lambda v: v.reduce_max(Mi[:, :], KT[0:64, :], axis=AX.X, apply_absolute_value=True),
                      reads=[("KTh", tg) for tg in range(8)], writes=["Mi"], tag=f"mi{h}")
                  for tg in range(4):
                      p.s(lambda s, tg=tg: s.activation(
                          absQ[:, :], QT[0:64, tg * 512:(tg + 1) * 512], AF.Abs),
                          reads=[("QTh", tg)], writes=["absQ"], tag=f"qab{h}{tg}")
                      p.t(lambda t, tg=tg: t.matmul(
                          ps_bd[:, :], Mi[:, :], absQ[:, :], start=True, stop=True),
                          reads=["Mi", "absQ"], writes=["ps_bd"], tag=f"bd{h}{tg}")
                      p.v(lambda v, tg=tg: v.tensor_scalar(
                          QT[64:65, tg * 512:(tg + 1) * 512], ps_bd[:, :], -1.0, None, op0=ALU.mult),
                          reads=["ps_bd"], writes=[("QTc", tg)], tag=f"bde{h}{tg}")
                  for qg in range(4):
                      for kh in range(2):
                          for kt_loc in range(16):
                              kt = kh * 16 + kt_loc
                              pp = kt % 2
                              p.t(lambda t, kt=kt, qg=qg, pp=pp: t.matmul(
                                  ps_st[pp][:, :], KT[:, kt * 128:(kt + 1) * 128],
                                  QT[:, qg * 512:(qg + 1) * 512], start=True, stop=True),
                                  reads=[("KTh", kt // 4), "KTones", ("QTh", qg), ("QTc", qg)],
                                  writes=[("ps_st", pp)], tag=f"st{h}{qg}{kt}")
                              p.s(lambda s, kt_loc=kt_loc, pp=pp: s.activation(
                                  P_sb[:, kt_loc * 512:(kt_loc + 1) * 512], ps_st[pp][:, :], AF.Exp),
                                  reads=[("ps_st", pp)], writes=[("P", kt_loc)], tag=f"ex{h}{qg}{kt}")
                          for kt_loc in range(16):
                              kt = kh * 16 + kt_loc
                              p.t(lambda t, kt=kt, kt_loc=kt_loc: t.matmul(
                                  ps_pv[:, :], Vau[:, kt * 65:(kt + 1) * 65],
                                  P_sb[:, kt_loc * 512:(kt_loc + 1) * 512],
                                  start=(kt == 0), stop=(kt == NCH - 1)),
                                  reads=[("Vau", kt), "Vau_ones", ("P", kt_loc)],
                                  writes=["ps_pv"], tag=f"pv{h}{qg}{kt}")
                      p.v(lambda v: v.tensor_copy(of_h[:, :], ps_pv[0:64, :]),
                          reads=["ps_pv"], writes=["of_h"], tag=f"pve{h}{qg}")
                      p.v(lambda v: v.tensor_copy(rs_sb[:, :], ps_pv[64:65, :]),
                          reads=["ps_pv"], writes=["rs_sb"], tag=f"rse{h}{qg}")
                      for tc4 in range(4):
                          p.t(lambda t, tc4=tc4: t.transpose(
                              ps_tr1[0:128, 0:64], of_h[:, tc4 * 128:(tc4 + 1) * 128], idb[0:64, 0:64]),
                              reads=["of_h", "idb"], writes=["ps_tr1"], tag=f"ot{h}{qg}{tc4}")
                          p.t(lambda t, tc4=tc4: t.transpose(
                              ps_rc[:, :], rs_sb[:, tc4 * 128:(tc4 + 1) * 128], idf[0:1, 0:1]),
                              reads=["rs_sb", "idf"], writes=["ps_rc"], tag=f"rt{h}{qg}{tc4}")
                          p.v(lambda v: v.reciprocal(rcol[:, :], ps_rc[:, :]),
                              reads=["ps_rc"], writes=["rcol"], tag=f"rc{h}{qg}{tc4}")
                          tcg = qg * 4 + tc4
                          p.v(lambda v, tcg=tcg, h=h: v.tensor_scalar(
                              out_n[:, tcg * D + h * 64: tcg * D + (h + 1) * 64],
                              ps_tr1[0:128, 0:64], rcol[:, :], None, op0=ALU.mult),
                              reads=["ps_tr1", "rcol"], writes=[("on", tcg, h)], tag=f"one{h}{qg}{tc4}")
        p.set_fence()
        if dbg:
            p.dbg_toks = []
            for tc in range(NTC):
                p.dbg_toks.append(p.sy(lambda s, tc=tc: s.dma_start(
                    out=on_dbg[tc * 128:(tc + 1) * 128, :], in_=out_n[:, tc * D:(tc + 1) * D]),
                    reads=[("on", tc, hh) for hh in range(16)], writes=[("ond", tc)],
                    dma=True, tag=f"ond{tc}"))

        if "q2" in phases:
          with ExitStack() as C2:
              def sb2(name, shape, dt): return C2.enter_context(nc.sbuf_tensor(name, shape, dt)).ap()
              def ps2_(name, shape, dt=F32): return C2.enter_context(nc.psum_tensor(name, shape, dt)).ap()
              on_fm = sb2("on_fm", [128, 8 * TH], BF16)
              Wu_bf = sb2("Wu_bf", [128, 8 * D], BF16)
              bu_bc = sb2("bu_bc", [128, D], F32)
              burow = sb2("burow", [1, D], F32)
              t1b = sb2("q2t1", [128, D], F32)
              ps_y = ps2_("ps_y2", [128, D])
              ps_tr2 = ps2_("ps_tr2", [128, 128], BF16)
              for dc in range(8):
                  p.g(lambda g, dc=dc: g.dma_start(
                      out=Wu_bf[:, dc * D:(dc + 1) * D], in_=wuni_in[dc * 128:(dc + 1) * 128, :]),
                      writes=[("Wu", dc)], dma=True, tag=f"ldWu{dc}")
              p.g(lambda g: g.dma_start(out=burow, in_=bu_in), writes=["burow"], dma=True, tag="ldbu")
              for half in range(2):
                  p.t(lambda t, half=half: t.matmul(
                      ps_y[:, half * 512:(half + 1) * 512], ones_f,
                      burow[:, half * 512:(half + 1) * 512], start=True, stop=True),
                      reads=["burow", "ones_f"], writes=["ps_y2"], tag=f"bubc{half}")
              p.v(lambda v: v.tensor_copy(bu_bc, ps_y),
                  reads=["ps_y2"], writes=["bu_bc"], tag="bubcev")
              for tc in range(NTC):
                  for dc in range(8):
                      p.t(lambda t, tc=tc, dc=dc: t.transpose(
                          ps_tr2[:, :], out_n[:, tc * D + dc * 128: tc * D + (dc + 1) * 128], idb),
                          reads=[("on", tc, 2 * dc), ("on", tc, 2 * dc + 1), "idb"],
                          writes=["ps_tr2b"], tag=f"q2t{tc}{dc}")
                      p.v(lambda v, tc=tc, dc=dc: v.tensor_copy(
                          on_fm[:, dc * TH + tc * 128: dc * TH + (tc + 1) * 128], ps_tr2[:, :]),
                          reads=["ps_tr2b"], writes=[("onfm", dc, tc)], tag=f"q2te{tc}{dc}")
              for tc in range(NTC):
                  for half in range(2):
                      for dc in range(8):
                          p.t(lambda t, tc=tc, dc=dc, half=half: t.matmul(
                              ps_y[:, half * 512:(half + 1) * 512],
                              on_fm[:, dc * TH + tc * 128: dc * TH + (tc + 1) * 128],
                              Wu_bf[:, dc * D + half * 512: dc * D + half * 512 + 512],
                              start=(dc == 0), stop=(dc == 7)),
                              reads=[("onfm", dc, tc), ("Wu", dc)], writes=["ps_y2"], tag=f"yu{tc}{half}{dc}")
                  p.v(lambda v: v.tensor_tensor(t1b[:, :], ps_y[:, :], bu_bc, op=ALU.add),
                      reads=["ps_y2", "bu_bc"], writes=["q2t1b"], tag=f"q2a{tc}")
                  p.v(lambda v, tc=tc: v.tensor_tensor(
                      x_tm[:, tc * D:(tc + 1) * D], x_tm[:, tc * D:(tc + 1) * D], t1b[:, :], op=ALU.add),
                      reads=["q2t1b", ("x", tc)], writes=[("x", tc)], tag=f"q2r{tc}")
              if dbg:
                  for tc in range(NTC):
                      p.dbg_toks.append(p.sy(lambda s, tc=tc: s.dma_start(
                          out=x3_dbg[tc * 128:(tc + 1) * 128, :], in_=x_tm[:, tc * D:(tc + 1) * D]),
                          reads=[("x", tc)], writes=[("x3d", tc)], dma=True, tag=f"x3d{tc}"))
        C12.close()
        p.set_fence()

        if "q3" in phases:
            emit_moe(p, nc, CTX, "l1f", x_tm, idb, idf, ones_f,
                     dict(kind="relu", wr_in=wr1_in, wg_in=None,
                          w1_in=w11_in, wu_in=None, w2_in=w21_in,
                          br_in=br1_in, b1_in=b11_in, b2_in=b21_in))
        p.set_fence()

        st_toks = []
        for tc in range(NTC):
            st_toks.append(p.sy(lambda s, tc=tc: s.dma_start(
                out=x4_out[tc * 128:(tc + 1) * 128, :], in_=x_tm[:, tc * D:(tc + 1) * D]),
                reads=[("x", tc)], dma=True, tag=f"stx4{tc}"))
        p.sy(lambda s: s.nop(), extra_deps=st_toks + getattr(p, "dbg_toks", []), tag="fin")

        p.finalize(nc, CTX)
        with nc.Block() as block:
            p.emit(block)
    return nc


_CACHE = {}

MASKS = {
    "maskL": np.tril(np.ones((128, 128), np.float32), -1),
    "maskU": np.triu(np.ones((128, 128), np.float32), 0),
    "idf": np.eye(128, dtype=np.float32),
    "idb": np.eye(128, dtype=np.float32).astype(BF),
    "consts": np.tile(np.asarray(CONST_VALS, np.float32)[None, :], (128, 1)),
}


def kernel(**inputs):
    ii = {k: np.asarray(v) for k, v in inputs.items()}
    if "l1" not in _CACHE:
        _CACHE["l1"] = build_launch1()
    nc1 = _CACHE["l1"]
    shared1 = {
        "Wq": ii["L0_attn_Wq"], "Wk": ii["L0_attn_Wk"], "Wv": ii["L0_attn_Wv"],
        "Wb": ii["L0_attn_Wb"],
        "convq": np.ascontiguousarray(ii["L0_attn_conv_q"].T),
        "convk": np.ascontiguousarray(ii["L0_attn_conv_k"].T),
        "convv": np.ascontiguousarray(ii["L0_attn_conv_v"].T),
        "g_rms": np.ascontiguousarray(ii["L0_attn_g_rms"][None, :]),
        "Wo": ii["L0_attn_Wo"],
        "ln_g": np.ascontiguousarray(ii["L0_attn_ln_g"][None, :]),
        "ln_b": np.ascontiguousarray(ii["L0_attn_ln_b"][None, :]),
        "w_router": ii["L0_ffn_w_router"], "w_gating": ii["L0_ffn_w_gating"],
        "w_gate": ii["L0_ffn_w_gate"], "w_up": ii["L0_ffn_w_up"],
        "w_down": ii["L0_ffn_w_down"],
        "W_down1": ii["L1_attn_W_down"],
        **MASKS,
    }
    x = ii["L0_attn_x"]
    in_maps1 = []
    for c in range(8):
        b, s = c // 2, c % 2
        in_maps1.append({
            "x_b": np.ascontiguousarray(x[b]),
            "x_half": np.ascontiguousarray(x[b, s * TH:(s + 1) * TH]),
            "s_sel": np.array([[s]], np.uint32),
            **shared1,
        })
    res1 = run_bass_kernel_spmd(nc1, in_maps1, core_ids=list(range(8)))
    x2 = [np.asarray(res1.results[c]["x2_half"]) for c in range(8)]
    xd = [np.asarray(res1.results[c]["xd_half"]) for c in range(8)]

    if "l2" not in _CACHE:
        _CACHE["l2"] = build_launch2()
    nc2 = _CACHE["l2"]
    shared2 = {
        "tokeysT": np.ascontiguousarray(np.transpose(ii["L1_attn_tokeys"], (0, 2, 1))),
        "toqueriesT": np.ascontiguousarray(np.transpose(ii["L1_attn_toqueries"], (0, 2, 1))),
        "tovaluesT": np.ascontiguousarray(np.transpose(ii["L1_attn_tovalues"], (0, 2, 1))),
        "W_unify": ii["L1_attn_W_unify"],
        "b_unify": np.ascontiguousarray(ii["L1_attn_b_unify"][None, :]),
        "Wr1": ii["L1_ffn_Wr"],
        "br1": np.ascontiguousarray(ii["L1_ffn_br"][None, :]),
        "W11": ii["L1_ffn_W1"],
        "b11": np.ascontiguousarray(ii["L1_ffn_b1"].reshape(E, 32, 128).transpose(0, 2, 1)),
        "W21": ii["L1_ffn_W2"], "b21": ii["L1_ffn_b2"],
        **MASKS,
    }
    in_maps2 = []
    for c in range(8):
        b, s = c // 2, c % 2
        prt = 2 * b + (1 - s)
        xd_perm = np.concatenate([xd[c], xd[prt]], axis=0)
        in_maps2.append({
            "xd_full": np.ascontiguousarray(xd_perm),
            "x2_half": x2[c],
            **shared2,
        })
    res2 = run_bass_kernel_spmd(nc2, in_maps2, core_ids=list(range(8)))
    out = np.empty((4, T, D), np.float32)
    for c in range(8):
        b, s = c // 2, c % 2
        out[b, s * TH:(s + 1) * TH] = np.asarray(res2.results[c]["x4_half"])
    return out
